# version 01
# baseline (speedup 1.0000x reference)
import os
import numpy as np
from contextlib import ExitStack

import concourse.bass as bass
import concourse.mybir as mybir
from concourse.bass_utils import run_bass_kernel_spmd

F32 = mybir.dt.float32
BF16 = mybir.dt.bfloat16
I32 = mybir.dt.int32
AF = mybir.ActivationFunctionType
OP = mybir.AluOpType
AX = mybir.AxisListType

C = 1024
H = 16
HD = 64
KC = 8
NCORE = 8
T_CORE = 4096
SEQ = 2048
TB = 512
NB = 8
L = 128
CPB = 4
NCH = 32
E = 32
EPC = 4
CAP = 1024
I_FF = 512
LN_EPS = 1e-5
GN_EPS = 0.00064
BIG = 65536.0
TOT = NCORE * T_CORE
CC_WAIT = 16 if (os.environ.get("BASS_SIM_FP64") or os.environ.get("BASS_SIM_COMPAT")) else 1

ENGINES = ("sync", "tensor", "vector", "scalar", "gpsimd")


NDQ = 8


class Prog:

    def __init__(self):
        self.ops = {e: [] for e in ENGINES}
        self.count = {e: 0 for e in ENGINES}
        self.dmacnt = {e: 0 for e in ENGINES}
        self.dquses = {}
        self.extra = {}
        self.last_w = {}
        self.readers = {}
        self.waited = {}

    def _need(self, waits, ceng, owner, val):
        if val <= 0:
            return
        k = (ceng, owner)
        if self.waited.get(k, 0) >= val:
            return
        self.waited[k] = val
        waits[owner] = max(waits.get(owner, 0), val)

    nrec = 0
    LIMIT = None

    def op(self, eng, fn, reads=(), writes=(), inc=1, sem_as=None):
        Prog.nrec += 1
        if Prog.LIMIT is not None and Prog.nrec > Prog.LIMIT:
            return
        waits = {}
        for b in reads:
            if b in self.last_w:
                ow, vv = self.last_w[b]
                self._need(waits, eng, ow, vv)
        for b in writes:
            if b in self.last_w:
                ow, vv = self.last_w[b]
                self._need(waits, eng, ow, vv)
            for ow, vv in self.readers.get(b, {}).items():
                self._need(waits, eng, ow, vv)
        if sem_as is not None:
            self.extra[sem_as] = self.extra.get(sem_as, 0) + CC_WAIT
            owner, val = sem_as, self.extra[sem_as]
        elif inc == 16:
            slot = self.dmacnt[eng] % NDQ
            self.dmacnt[eng] += 1
            key = (eng, slot)
            prev = self.dquses.get(key, 0)
            if prev:
                self._need(waits, eng, f"{eng}_dq{slot}", prev * 16)
            self.dquses[key] = prev + 1
            owner, val = f"{eng}_dq{slot}", (prev + 1) * 16
        else:
            self.count[eng] += inc
            owner, val = eng, self.count[eng]
        self.ops[eng].append((waits, fn, owner))
        for b in writes:
            self.last_w[b] = (owner, val)
            self.readers[b] = {}
        for b in reads:
            self.readers.setdefault(b, {})[owner] = val

    def emit(self, block, handles):
        def sem(owner):
            return handles[owner + "_sem"]
        for eng in ENGINES:
            def make(eng=eng, oplist=self.ops[eng]):
                def body(h):
                    for waits, fn, owner in oplist:
                        for ow in sorted(waits):
                            h.wait_ge(sem(ow), waits[ow])
                        inst = fn(h)
                        if inst is not None:
                            if owner in self.extra:
                                inst.then_inc(sem(owner))
                            else:
                                inst.then_inc(sem(owner), 16 if "_dq" in owner else 1)
                    if eng == "sync":
                        for oe in ENGINES:
                            if oe != "sync" and self.count[oe]:
                                h.wait_ge(sem(oe), self.count[oe])
                        for name, cnt in self.extra.items():
                            h.wait_ge(sem(name), cnt)
                        for (qe, slot), uses in self.dquses.items():
                            h.wait_ge(sem(f"{qe}_dq{slot}"), uses * 16)
                        if self.count["sync"]:
                            h.wait_ge(sem("sync"), self.count["sync"])
                return body
            getattr(block, eng)(make())


def build_nc(n_blocks=NB, do_ffn=True):
    nc = bass.Bass("TRN2", target_bir_lowering=False)
    dram = {}

    def din(name, shape, dt=F32):
        dram[name] = nc.dram_tensor(name, shape, dt, kind="ExternalInput").ap()
        return dram[name]

    def dint(name, shape, dt=F32):
        dram[name] = nc.dram_tensor(name, shape, dt, kind="Internal").ap()
        return dram[name]

    x_in = din("x", [T_CORE, C])
    for v in ("maa_x", "maa_w", "maa_k", "maa_v", "maa_r", "t_decay", "ln_w", "ln_b"):
        din(v, [C])
    din("maa_w1", [C, 128])
    din("maa_w2", [4, 32, C])
    din("dw1", [C, 64])
    din("dw2", [64, C])
    din("faaaa", [H, HD])
    for v in ("W_r", "W_k", "W_v", "W_o"):
        din(v, [C, C])
    din("gate_w", [E, C])
    din("wgu_own", [EPC, C, 2 * I_FF])
    din("wd_own", [EPC, I_FF, C])
    din("wsgu", [C, 2 * I_FF])
    din("wsd", [I_FF, C])
    din("meta_ind", [8, 1])
    din("meta_cols", [128, 8])
    y_out = nc.dram_tensor("y", [T_CORE, C], F32, kind="ExternalOutput").ap()

    xln_d = dint("xln_d", [T_CORE, C], BF16)
    yn_d = dint("yn_d", [T_CORE, C], BF16)
    kbar_d = dint("kbar_d", [C, TB], BF16)
    xn_d = dint("xn_d", [T_CORE, C], BF16)
    xn_all = dint("xn_all", [TOT + 128, C], BF16)
    cnt_d = dint("cnt_d", [1, E])
    cnt8_d = dint("cnt8_d", [NCORE, E])
    slotmap4_d = [dint(f"slotmap{kk}_d", [E * CAP + 128, 2]) for kk in range(4)]
    slotmapG4_d = [dint(f"slotmapG{kk}_d", [E * CAP, 2]) for kk in range(4)]
    cap_d = dint("cap_d", [EPC * CAP, C], BF16)
    eout_d = dint("eout_d", [EPC * CAP, C], BF16)
    eoutG_d = dint("eoutG_d", [E * CAP + 128, C], BF16)
    sh_d = dint("sh_d", [T_CORE, C], BF16)
    x2_d = dint("x2_d", [T_CORE, C])

    p = Prog()
    handles = {}
    es = ExitStack()

    def sb(name, shape, dt):
        t = es.enter_context(nc.sbuf_tensor(name, shape, dt))
        handles[name] = t
        return t

    def psum(name, shape, dt=F32):
        t = es.enter_context(nc.psum_tensor(name, shape, dt))
        handles[name] = t
        return t

    with es:
        for e in ENGINES:
            handles[e + "_sem"] = es.enter_context(nc.semaphore(e + "_sem"))
        for cc in ("ag0", "ag1", "ar1", "ag2"):
            handles[cc + "_sem"] = es.enter_context(nc.semaphore(cc + "_sem"))
        for e in ("sync", "scalar", "gpsimd"):
            for i in range(NDQ):
                handles[f"{e}_dq{i}_sem"] = es.enter_context(
                    nc.semaphore(f"{e}_dq{i}_sem"))

        vecs = sb("vecs", [128, 8 * KC], F32)
        VEC = {n: i for i, n in enumerate(
            ("maa_x", "maa_w", "maa_k", "maa_v", "maa_r", "t_decay", "ln_w", "ln_b"))}

        def vcol(vn, k):
            i = VEC[vn] * KC + k
            return vecs[:, i:i + 1]

        ident_f = sb("ident_f", [128, 128], F32)
        trilmask = sb("trilmask", [128, 128], BF16)
        stril_f = sb("stril_f", [128, 128], F32)
        ones1b = sb("ones1b", [1, 128], BF16)
        ones1f = sb("ones1f", [1, 128], F32)
        onescol_b = sb("onescol_b", [128, 1], BF16)
        onescol_f = sb("onescol_f", [128, 1], F32)
        ublk = sb("ublk", [128, H], BF16)
        iotak = sb("iotak", [128, E], F32)
        iotak1024 = sb("iotak1024", [128, E], F32)
        iota_tg1 = sb("iota_tg1", [128, NCH], F32)
        iotasch = sb("iotasch", [128, 8], F32)
        trash_iota = sb("trash_iota", [128, NCH * 4], F32)
        scanmask = sb("scanmask", [128, TB], F32)
        meta_sb = sb("meta_sb", [128, 8], F32)
        consts = sb("consts", [128, 4], F32)
        ind_sb = sb("ind_sb", [8, 1], F32)
        zero_sb = sb("zero_sb", [128, 512], F32)

        w_r = sb("w_r", [128, KC * C], BF16)
        w_k = sb("w_k", [128, KC * C], BF16)
        w_v = sb("w_v", [128, KC * C], BF16)
        w_o = sb("w_o", [128, KC * C], BF16)
        w_m1 = sb("w_m1", [128, KC * 128], BF16)
        w_m2 = sb("w_m2", [64, 2 * C], BF16)
        w_d1 = sb("w_d1", [128, KC * 64], BF16)
        w_d2 = sb("w_d2", [64, C], BF16)
        gwT = sb("gwT", [128, KC * E], BF16)
        btil = sb("btil", [1, C], BF16)
        lnbstage = sb("lnbstage", [128, KC], BF16)

        x_tok = sb("x_tok", [128, C], F32)
        bn6 = sb("bn6", [128, 12], F32)
        bn2 = sb("bn2", [128, 2], F32)
        mcol = sb("mcol", [128, 1], F32)
        rstdcol = sb("rstdcol", [128, 1], F32)
        tcol = sb("tcol", [128, 1], F32)
        xln_tok = sb("xln_tok", [128, C], BF16)
        W1 = 544
        HC = 15
        D0 = 16
        xlnfm = sb("xlnfm", [128, KC * W1], BF16)
        xxfm = sb("xxfm", [128, KC * TB], BF16)
        xxx_t = sb("xxx_t", [128, 2 * TB], BF16)
        af_t = sb("af_t", [128, 2 * TB], BF16)
        xf_t = sb("xf_t", [128, KC * TB], BF16)
        tanhA = sb("tanhA", [64, TB], BF16)
        tanhB = sb("tanhB", [64, TB], BF16)
        tanhw = sb("tanhw", [64, TB], BF16)
        p_t = sb("p_t", [128, 2 * TB], F32)
        incl_t = sb("incl_t", [128, 2 * TB], F32)
        rs_t = sb("rs_t", [128, TB], BF16)
        ks_t = sb("ks_t", [128, TB], BF16)
        dend = sb("dend", [128, KC * CPB], F32)
        r_fm = sb("r_fm", [128, KC * TB], BF16)
        k_fm = sb("k_fm", [128, KC * TB], BF16)
        ruk_fm = sb("ruk_fm", [128, KC * TB], BF16)
        kb_fm = sb("kb_fm", [128, 2 * TB], BF16)
        kb_tok = sb("kb_tok", [128, 2 * C], BF16)
        v_tok = sb("v_tok", [128, CPB * C], BF16)
        abuf = sb("abuf", [128, 4 * 512], BF16)
        d_sb = sb("d_sb", [128, H], BF16)
        dv_t = sb("dv_t", [128, C], BF16)
        y_sb = sb("y_sb", [128, C], F32)
        gn16a = sb("gn16a", [128, H], F32)
        gn16b = sb("gn16b", [128, H], F32)
        gnal = sb("gnal", [128, H], F32)
        gnbe = sb("gnbe", [128, H], F32)
        yn_tok = sb("yn_tok", [128, C], BF16)
        ynfm = sb("ynfm", [128, KC * TB], BF16)
        S_sb = sb("S_sb", [128, 8 * HD], F32)
        S_bf = sb("S_bf", [128, 8 * HD], BF16)
        Sdec = sb("Sdec", [128, 8 * HD], F32)
        ao_sb = sb("ao_sb", [128, C], F32)
        aoacc = sb("aoacc", [128, 4], F32)
        sqscr = sb("sqscr", [128, 512], F32)
        x2_sb = sb("x2_sb", [128, C], F32)
        xn_tok = xln_tok
        h_fm = xxfm
        NROUT = 4
        rout_t = [xf_t[:, i * C:(i + 1) * C] for i in range(NROUT)]
        fin_t = y_sb
        cnt8_sb = sb("cnt8_sb", [8, E], F32)
        cntrow = sb("cntrow", [1, E], F32)
        maskf = sb("maskf", [128, E], F32)
        smg_all = sb("smg_all", [128, EPC * 8], F32)
        capidx_all = sb("capidx_all", [128, EPC * 8], I32)
        smrows_all = sb("smrows_all", [128, EPC * 8 * 4 * 2], F32)
        capf2_all = sb("capf2_all", [128, EPC * 8 * 2], F32)
        capf = sb("capf", [128, EPC * 8], F32)
        ustage = sb("ustage", [16, HD], F32)

        vecstage = y_sb[0:8, 0:1024]
        gstage = x2_sb[0:32, 0:1024]
        chpfx = ao_sb[0:1, 0:NCH * E]
        rf32 = r_fm.ap().bitcast(F32)
        kf32 = k_fm.ap().bitcast(F32)
        rkf32 = ruk_fm.ap().bitcast(F32)
        vtf32 = v_tok.ap().bitcast(F32)
        logits_sb = rf32[:, 0:NCH * E]
        m8_sb = rf32[:, 1024:1024 + NCH * 8]
        thrT = rf32[:, 1280:1280 + NCH]
        m1T = rf32[:, 1312:1312 + NCH]
        posg_sb = kf32[:, 0:NCH * E]
        gates_sb = kf32[:, 1024:2048]
        tbl_sb = rkf32[:, 0:NCH * E]
        keep_sb = rkf32[:, 1024:2048]
        maskb = v_tok[:, 0:NCH * E]
        slotcols = vtf32[:, 512:512 + NCH * 4]
        gcols = vtf32[:, 640:640 + NCH * 4]
        sc_all = vtf32[:, 768:768 + NCH * 8]
        slotints = vtf32[:, 1024:1024 + NCH * 4].bitcast(I32)

        w_gu_e = w_r
        w_d_e = w_k
        capfm = w_v
        w_sgu_a = w_o
        w_sd_a = xlnfm

        pbb = [psum("pb0", [128, 512], F32), psum("pb1", [128, 512], F32)]
        pbA = [psum("pbA0", [128, 512], F32), psum("pbA1", [128, 512], F32)]
        pbY = [psum("pbY0", [128, 512], F32), psum("pbY1", [128, 512], F32)]
        pbS = psum("pbS", [128, 512], F32)
        pbsm = psum("pbsm", [128, 512], F32)

        block = es.enter_context(nc.Block())

        p.op("gpsimd", lambda g: g.memset(ident_f[:, :], 1.0), writes=("ident_f",))
        p.op("gpsimd", lambda g: g.affine_select(
            ident_f[:, :], ident_f[:, :], pattern=[[1, 128]], compare_op=OP.is_equal,
            fill=0.0, base=0, channel_multiplier=-1),
            reads=("ident_f",), writes=("ident_f",))
        p.op("gpsimd", lambda g: g.memset(trilmask[:, :], 1.0), writes=("trilmask",))
        p.op("gpsimd", lambda g: g.affine_select(
            trilmask[:, :], trilmask[:, :], pattern=[[1, 128]], compare_op=OP.is_gt,
            fill=0.0, base=0, channel_multiplier=-1),
            reads=("trilmask",), writes=("trilmask",))
        p.op("gpsimd", lambda g: g.memset(stril_f[:, :], 1.0), writes=("stril_f",))
        p.op("gpsimd", lambda g: g.affine_select(
            stril_f[:, :], stril_f[:, :], pattern=[[1, 128]], compare_op=OP.is_gt,
            fill=0.0, base=0, channel_multiplier=-1),
            reads=("stril_f",), writes=("stril_f",))
        for tile, key in ((ones1b, "ones1b"), (ones1f, "ones1f"),
                          (onescol_b, "onescol_b"), (onescol_f, "onescol_f")):
            p.op("gpsimd", lambda g, tile=tile: g.memset(tile[:, :], 1.0), writes=(key,))
        p.op("gpsimd", lambda g: g.memset(zero_sb[:, :], 0.0), writes=("zero_sb",))
        for ci, cval in enumerate((LN_EPS, GN_EPS, 0.0, 1.0)):
            p.op("gpsimd", lambda g, ci=ci, cval=cval: g.memset(
                consts[:, ci:ci + 1], cval), writes=("consts",))
            nc.const_aps.register(F32, cval, consts[:, ci:ci + 1])
        p.op("gpsimd", lambda g: g.iota(
            iotak[:, :], pattern=[[CAP, E]], base=0, channel_multiplier=0,
            allow_small_or_imprecise_dtypes=True), writes=("iotak",))
        p.op("gpsimd", lambda g: g.iota(
            iotak1024[:, :], pattern=[[CAP, E]], base=CAP, channel_multiplier=0,
            allow_small_or_imprecise_dtypes=True), writes=("iotak1024",))
        p.op("gpsimd", lambda g: g.iota(
            iota_tg1[:, :], pattern=[[128, NCH]], base=1, channel_multiplier=1,
            allow_small_or_imprecise_dtypes=True), writes=("iota_tg1",))
        p.op("gpsimd", lambda g: g.iota(
            iotasch[:, :], pattern=[[128, 8]], base=0, channel_multiplier=1,
            allow_small_or_imprecise_dtypes=True), writes=("iotasch",))
        p.op("gpsimd", lambda g: g.iota(
            trash_iota[:, :], pattern=[[1, NCH * 4]], base=E * CAP,
            channel_multiplier=0, allow_small_or_imprecise_dtypes=True),
            writes=("trash_iota",))
        p.op("gpsimd", lambda g: g.memset(scanmask[:, :], 1.0), writes=("scanmask",))
        p.op("gpsimd", lambda g: g.memset(
            scanmask.ap().rearrange("q (a b) -> q a b", b=L)[:, :, 0:1], 0.0),
            reads=("scanmask",), writes=("scanmask",))

        p.op("sync", lambda s: s.dma_start(meta_sb[:, :], dram["meta_cols"]),
             writes=("meta_sb",), inc=16)
        p.op("sync", lambda s: s.dma_start(ind_sb[:, :], dram["meta_ind"]),
             writes=("ind_sb",), inc=16)
        p.op("vector", lambda v: v.tensor_scalar(
            iota_tg1[:, :], iota_tg1[:, :], meta_sb[:, 0:1], None, OP.add),
            reads=("iota_tg1", "meta_sb"), writes=("iota_tg1",))

        for vi, vn in enumerate(VEC):
            src = dram[vn].rearrange("(a b) -> a b", b=128)
            p.op("sync", lambda s, src=src, vi=vi: s.dma_start(
                vecstage[:, vi * 128:(vi + 1) * 128], src),
                writes=("y_sb",), inc=16)
        for vi in range(8):
            p.op("tensor", lambda t, vi=vi: t.transpose(
                pbsm[:, vi * 8: vi * 8 + 8], vecstage[0:8, vi * 128:(vi + 1) * 128],
                ident_f[0:8, 0:8]),
                reads=("y_sb", "ident_f"), writes=("pbsm",))
        p.op("vector", lambda v: v.tensor_copy(vecs[:, 0:64], pbsm[:, 0:64]),
             reads=("pbsm",), writes=("vecs",))

        p.op("sync", lambda s: s.dma_start(ustage[:, :], dram["faaaa"]),
             writes=("ustage",), inc=16)
        p.op("tensor", lambda t: t.transpose(
            pbsm[0:64, 64:80], ustage[0:16, :], ident_f[0:16, 0:16]),
            reads=("ustage", "ident_f"), writes=("pbsm",))
        p.op("gpsimd", lambda g: g.memset(ublk[:, :], 0.0), writes=("ublk",))
        for g_ in range(8):
            p.op("vector", lambda v, g_=g_: v.tensor_copy(
                ublk[0:64, 2 * g_:2 * g_ + 1], pbsm[0:64, 64 + 2 * g_:65 + 2 * g_]),
                reads=("pbsm",), writes=("ublk",))
            p.op("vector", lambda v, g_=g_: v.tensor_copy(
                ublk[64:128, 2 * g_ + 1:2 * g_ + 2], pbsm[0:64, 65 + 2 * g_:66 + 2 * g_]),
                reads=("pbsm",), writes=("ublk",))

        def wload(dst, src_name, cols, key):
            src = dram[src_name].rearrange("(a b) c -> b a c", b=128)
            p.op("gpsimd", lambda g: g.dma_start(
                dst.ap().rearrange("q (a c) -> q a c", c=cols), src),
                writes=(key,), inc=16)

        wload(w_r, "W_r", C, "w_r")
        wload(w_k, "W_k", C, "w_k")
        wload(w_v, "W_v", C, "w_v")
        wload(w_o, "W_o", C, "w_o")
        wload(w_m1, "maa_w1", 128, "w_m1")
        wload(w_d1, "dw1", 64, "w_d1")
        p.op("gpsimd", lambda g: g.dma_start(w_d2[:, :], dram["dw2"]),
             writes=("w_d2",), inc=16)
        p.op("gpsimd", lambda g: g.dma_start(
            w_m2[0:32, :].rearrange("q (a c) -> q a c", a=2),
            dram["maa_w2"].rearrange("(a two) q c -> q a c", two=2)[:, :, 0:C]
            if False else
            dram["maa_w2"].rearrange("(a two) q c -> two q a c", two=2)[0]),
            writes=("w_m2",), inc=16)
        p.op("gpsimd", lambda g: g.dma_start(
            w_m2[32:64, :].rearrange("q (a c) -> q a c", a=2),
            dram["maa_w2"].rearrange("(a two) q c -> two q a c", two=2)[1]),
            writes=("w_m2",), inc=16)

        p.op("sync", lambda s: s.dma_start(gstage[:, :], dram["gate_w"]),
             writes=("x2_sb",), inc=16)
        for k in range(KC):
            p.op("tensor", lambda t, k=k: t.transpose(
                pbsm[:, 128:160], gstage[0:32, k * 128:(k + 1) * 128],
                ident_f[0:32, 0:32]),
                reads=("x2_sb", "ident_f"), writes=("pbsm",))
            p.op("vector", lambda v, k=k: v.tensor_copy(
                gwT[:, k * E:(k + 1) * E], pbsm[:, 128:160]),
                reads=("pbsm",), writes=("gwT",))

        p.op("vector", lambda v: v.tensor_copy(
            lnbstage[:, :], vecs[:, VEC["ln_b"] * KC:(VEC["ln_b"] + 1) * KC]),
            reads=("vecs",), writes=("lnbstage",))
        for half in range(2):
            def fnB(t, half=half):
                inst = None
                for k in range(KC):
                    inst = t.matmul(
                        pbsm[0:1, 0:512], lnbstage[:, k:k + 1],
                        w_o[:, k * C + half * 512: k * C + (half + 1) * 512],
                        start=(k == 0), stop=(k == KC - 1))
                return inst
            p.op("tensor", fnB, reads=("lnbstage", "w_o"), writes=("pbsm",))
            p.op("vector", lambda v, half=half: v.tensor_copy(
                btil[0:1, half * 512:(half + 1) * 512], pbsm[0:1, 0:512]),
                reads=("pbsm",), writes=("btil",))
        for k in range(KC):
            p.op("vector", lambda v, k=k: v.tensor_scalar(
                w_o[:, k * C:(k + 1) * C], w_o[:, k * C:(k + 1) * C],
                vcol("ln_w", k), None, OP.mult),
                reads=("w_o", "vecs", "btil"), writes=("w_o",))

        for kk in range(4):
            p.op("sync", lambda s, kk=kk: s.dma_start(
                slotmap4_d[kk][0:E * CAP, :]
                .rearrange("(a q c2) c -> a q (c2 c)", q=128, c2=256)[0],
                zero_sb[:, :]), reads=("zero_sb",), writes=(f"smz{kk}",), inc=16)
            p.op("sync", lambda s, kk=kk: s.dma_start(
                slotmap4_d[kk][E * CAP:E * CAP + 128, :], zero_sb[:, 0:2]),
                reads=("zero_sb",), writes=(f"smz{kk}",), inc=16)
        for half in range(2):
            p.op("gpsimd", lambda g, half=half: g.dma_start(
                xn_all[TOT:TOT + 128, half * 512:(half + 1) * 512], zero_sb[:, :]),
                reads=("zero_sb",), writes=("xnall_z",), inc=16)
            p.op("gpsimd", lambda g, half=half: g.dma_start(
                eoutG_d[E * CAP:E * CAP + 128, half * 512:(half + 1) * 512],
                zero_sb[:, :]),
                reads=("zero_sb",), writes=("eoutG_z",), inc=16)

        for B in range(n_blocks):
            tok0 = B * TB
            seq_start = (B % (SEQ // TB)) == 0

            for j in range(CPB):
                ch = B * CPB + j
                rows = x_in.rearrange("(a q) c -> a q c", q=128)[ch]
                p.op("sync", lambda s, rows=rows: s.dma_start(x_tok[:, :], rows),
                     writes=("x_tok",), inc=16)
                p.op("vector", lambda v: v.bn_stats(bn6[:, 0:6], x_tok[:, 0:512]),
                     reads=("x_tok",), writes=("bn6",))
                p.op("vector", lambda v: v.bn_stats(bn6[:, 6:12], x_tok[:, 512:1024]),
                     reads=("x_tok",), writes=("bn6",))
                p.op("vector", lambda v: v.bn_aggr(bn2[:, :], bn6[:, 0:12]),
                     reads=("bn6",), writes=("bn2",))
                p.op("scalar", lambda s: s.activation(
                    rstdcol[:, :], bn2[:, 1:2], AF.Sqrt, bias=LN_EPS, scale=1.0),
                    reads=("bn2", "consts"), writes=("rstdcol",))
                p.op("vector", lambda v: v.reciprocal(rstdcol[:, :], rstdcol[:, :]),
                     reads=("rstdcol",), writes=("rstdcol",))
                p.op("vector", lambda v: v.tensor_scalar(
                    xln_tok[:, :], x_tok[:, :], bn2[:, 0:1], rstdcol[:, :],
                    OP.subtract, OP.mult),
                    reads=("x_tok", "bn2", "rstdcol"), writes=("xln_tok",))
                rows_o = xln_d.rearrange("(a q) c -> a q c", q=128)[ch]
                p.op("sync", lambda s, rows_o=rows_o: s.dma_start(rows_o, xln_tok[:, :]),
                     reads=("xln_tok",), writes=("xln_dram",), inc=16)

            for k in range(KC):
                if seq_start:
                    p.op("gpsimd", lambda g, k=k: g.memset(
                        xlnfm[:, k * W1 + HC:k * W1 + HC + 1], 0.0),
                        reads=("xlnfm",), writes=("xlnfm",))
                else:
                    p.op("vector", lambda v, k=k: v.tensor_copy(
                        xlnfm[:, k * W1 + HC:k * W1 + HC + 1],
                        xlnfm[:, k * W1 + D0 + TB - 1:k * W1 + D0 + TB]),
                        reads=("xlnfm",), writes=("xlnfm",))
            for k in range(KC):
                src = xln_d[tok0:tok0 + TB, k * 128:(k + 1) * 128]
                p.op("sync", lambda s, src=src, k=k: s.dma_start(
                    xlnfm[:, k * W1 + D0:k * W1 + D0 + TB], src, transpose=True),
                    reads=("xln_dram",), writes=("xlnfm",), inc=16)

            for k in range(KC):
                p.op("vector", lambda v, k=k: v.tensor_sub(
                    xxfm[:, k * TB:(k + 1) * TB],
                    xlnfm[:, k * W1 + HC:k * W1 + HC + TB],
                    xlnfm[:, k * W1 + D0:k * W1 + D0 + TB]),
                    reads=("xlnfm",), writes=("xxfm",))

            for k in range(KC):
                pk = k % 2
                p.op("vector", lambda v, k=k, pk=pk: v.scalar_tensor_tensor(
                    xxx_t[:, pk * TB:(pk + 1) * TB],
                    xxfm[:, k * TB:(k + 1) * TB], vcol("maa_x", k),
                    xlnfm[:, k * W1 + D0:k * W1 + D0 + TB],
                    OP.mult, OP.add),
                    reads=("xxfm", "vecs", "xlnfm"), writes=(f"xxx{pk}",))
                p.op("tensor", lambda t, k=k, pk=pk: t.matmul(
                    pbb[0][:, :], w_m1[:, k * 128:(k + 1) * 128],
                    xxx_t[:, pk * TB:(pk + 1) * TB],
                    start=(k == 0), stop=(k == KC - 1)),
                    reads=(f"xxx{pk}", "w_m1"), writes=("pb0",))
            p.op("scalar", lambda s: s.activation(
                tanhA[:, :], pbb[0][0:64, :], AF.Tanh), reads=("pb0",), writes=("tanhA",))
            p.op("scalar", lambda s: s.activation(
                tanhB[:, :], pbb[0][64:128, :], AF.Tanh), reads=("pb0",), writes=("tanhB",))

            tanh_f = [(tanhA, 0, "tanhA"), (tanhA, 32, "tanhA"),
                      (tanhB, 0, "tanhB"), (tanhB, 32, "tanhB")]

            for fi, fname in ((1, "k"), (2, "v"), (3, "r"), (0, "w")):
                ttile, trow, tkey = tanh_f[fi]
                vkey = {"w": "maa_w", "k": "maa_k", "v": "maa_v", "r": "maa_r"}[fname]
                for k in range(KC):
                    pk = k % 2
                    p.op("tensor", lambda t, k=k, ttile=ttile, trow=trow, fi=fi: t.matmul(
                        pbb[1][:, :],
                        w_m2[32 * (fi % 2):32 * (fi % 2) + 32,
                             (fi // 2) * C + k * 128: (fi // 2) * C + (k + 1) * 128],
                        ttile[trow:trow + 32, :], start=True, stop=True),
                        reads=(tkey, "w_m2"), writes=("pb1",))
                    p.op("scalar", lambda s, k=k, pk=pk, vkey=vkey: s.activation(
                        af_t[:, pk * TB:(pk + 1) * TB], pbb[1][:, :], AF.Identity,
                        bias=vcol(vkey, k), scale=1.0),
                        reads=("pb1", "vecs"), writes=(f"af{pk}",))
                    p.op("vector", lambda v, k=k, pk=pk: v.tensor_mul(
                        xf_t[:, k * TB:(k + 1) * TB], xxfm[:, k * TB:(k + 1) * TB],
                        af_t[:, pk * TB:(pk + 1) * TB]),
                        reads=(f"af{pk}", "xxfm"), writes=("xf",))
                    p.op("gpsimd", lambda g, k=k: g.tensor_add(
                        xf_t[:, k * TB:(k + 1) * TB], xf_t[:, k * TB:(k + 1) * TB],
                        xlnfm[:, k * W1 + D0:k * W1 + D0 + TB]),
                        reads=("xf", "xlnfm"), writes=("xf",))

                if fname == "w":
                    def fn_dw1(t):
                        inst = None
                        for k in range(KC):
                            inst = t.matmul(pbb[0][0:64, :],
                                            w_d1[:, k * 64:(k + 1) * 64],
                                            xf_t[:, k * TB:(k + 1) * TB],
                                            start=(k == 0), stop=(k == KC - 1))
                        return inst
                    p.op("tensor", fn_dw1, reads=("xf", "w_d1"), writes=("pb0",))
                    p.op("scalar", lambda s: s.activation(
                        tanhw[:, :], pbb[0][0:64, :], AF.Tanh),
                        reads=("pb0",), writes=("tanhw",))
                    for k in range(KC):
                        pk = k % 2
                        p.op("tensor", lambda t, k=k: t.matmul(
                            pbb[0][:, :], w_d2[:, k * 128:(k + 1) * 128], tanhw[:, :],
                            start=True, stop=True),
                            reads=("tanhw", "w_d2"), writes=("pb0",))
                        p.op("scalar", lambda s, k=k, pk=pk: s.activation(
                            p_t[:, pk * TB:(pk + 1) * TB], pbb[0][:, :], AF.Exp,
                            bias=vcol("t_decay", k), scale=1.0),
                            reads=("pb0", "vecs"), writes=(f"p{pk}",))
                        p.op("vector", lambda v, pk=pk: v.tensor_tensor_scan(
                            incl_t[:, pk * TB:(pk + 1) * TB], scanmask[:, :],
                            p_t[:, pk * TB:(pk + 1) * TB], 0.0, OP.mult, OP.add),
                            reads=(f"p{pk}", "scanmask"), writes=(f"incl{pk}",))
                        p.op("scalar", lambda s, pk=pk: s.activation(
                            ks_t[:, :], incl_t[:, pk * TB:(pk + 1) * TB], AF.Exp),
                            reads=(f"incl{pk}",), writes=("ks_t",))
                        p.op("scalar", lambda s, k=k, pk=pk: s.activation(
                            dend[:, k * CPB:(k + 1) * CPB],
                            incl_t[:, pk * TB:(pk + 1) * TB]
                            .rearrange("q (a b) -> q a b", b=L)[:, :, L - 1:L].squeeze(2),
                            AF.Exp, bias=0.0, scale=-1.0),
                            reads=(f"incl{pk}", "consts"), writes=("dend",))
                        p.op("vector", lambda v, pk=pk: v.tensor_sub(
                            p_t[:, pk * TB:(pk + 1) * TB], p_t[:, pk * TB:(pk + 1) * TB],
                            incl_t[:, pk * TB:(pk + 1) * TB]),
                            reads=(f"p{pk}", f"incl{pk}"), writes=(f"p{pk}",))
                        p.op("scalar", lambda s, pk=pk: s.activation(
                            rs_t[:, :], p_t[:, pk * TB:(pk + 1) * TB], AF.Exp),
                            reads=(f"p{pk}",), writes=("rs_t",))
                        p.op("vector", lambda v, k=k: v.tensor_mul(
                            ruk_fm[:, k * TB:(k + 1) * TB], r_fm[:, k * TB:(k + 1) * TB],
                            k_fm[:, k * TB:(k + 1) * TB]),
                            reads=("r_fm", "k_fm"), writes=("ruk_fm",))
                        p.op("gpsimd", lambda g, k=k, pk=pk: g.tensor_mul(
                            kb_fm[:, pk * TB:(pk + 1) * TB],
                            k_fm[:, k * TB:(k + 1) * TB], ks_t[:, :]),
                            reads=("k_fm", "ks_t"), writes=("kb_fm",))
                        p.op("gpsimd", lambda g, k=k, pk=pk: g.tensor_mul(
                            kb_fm[:, pk * TB:(pk + 1) * TB]
                            .rearrange("q (a b) -> q a b", b=L),
                            kb_fm[:, pk * TB:(pk + 1) * TB]
                            .rearrange("q (a b) -> q a b", b=L),
                            dend[:, k * CPB:(k + 1) * CPB].unsqueeze(2)
                            .to_broadcast([128, CPB, L])),
                            reads=("kb_fm", "dend"), writes=("kb_fm",))
                        p.op("sync", lambda s, k=k, pk=pk: s.dma_start(
                            kbar_d[k * 128:(k + 1) * 128, :],
                            kb_fm[:, pk * TB:(pk + 1) * TB]),
                            reads=("kb_fm",), writes=("kbar_dram",), inc=16)
                        p.op("vector", lambda v, k=k: v.tensor_mul(
                            k_fm[:, k * TB:(k + 1) * TB], k_fm[:, k * TB:(k + 1) * TB],
                            ks_t[:, :]),
                            reads=("k_fm", "ks_t"), writes=("k_fm",))
                        p.op("vector", lambda v, k=k: v.tensor_mul(
                            r_fm[:, k * TB:(k + 1) * TB], r_fm[:, k * TB:(k + 1) * TB],
                            rs_t[:, :]),
                            reads=("r_fm", "rs_t"), writes=("r_fm",))
                elif fname in ("k", "r"):
                    wmat = w_k if fname == "k" else w_r
                    wkey = "w_k" if fname == "k" else "w_r"
                    dst = k_fm if fname == "k" else r_fm
                    dkey = "k_fm" if fname == "k" else "r_fm"
                    for o in range(KC):
                        po = o % 2
                        pt = pbb[po]

                        def fn(t, o=o, pt=pt, wmat=wmat):
                            inst = None
                            for k in range(KC):
                                inst = t.matmul(
                                    pt[:, :],
                                    wmat[:, k * C + o * 128: k * C + o * 128 + 128],
                                    xf_t[:, k * TB:(k + 1) * TB],
                                    start=(k == 0), stop=(k == KC - 1))
                            return inst
                        p.op("tensor", fn, reads=("xf", wkey), writes=(f"pb{po}",))
                        p.op("scalar", lambda s, o=o, pt=pt, dst=dst: s.activation(
                            dst[:, o * TB:(o + 1) * TB], pt[:, :], AF.Copy),
                            reads=(f"pb{po}",), writes=(dkey,))
                else:
                    for j in range(CPB):
                        for half in range(2):
                            pt = pbb[half]

                            def fn(t, j=j, half=half, pt=pt):
                                inst = None
                                for k in range(KC):
                                    inst = t.matmul(
                                        pt[:, :],
                                        xf_t[:, k * TB + j * 128: k * TB + (j + 1) * 128],
                                        w_v[:, k * C + half * 512: k * C + (half + 1) * 512],
                                        start=(k == 0), stop=(k == KC - 1))
                                return inst
                            p.op("tensor", fn, reads=("xf", "w_v"), writes=(f"pb{half}",))
                            p.op("scalar", lambda s, j=j, half=half, pt=pt: s.activation(
                                v_tok[:, j * C + half * 512: j * C + (half + 1) * 512],
                                pt[:, :], AF.Copy),
                                reads=(f"pb{half}",), writes=("v_tok",))

            if seq_start:
                p.op("gpsimd", lambda g: g.memset(S_sb[:, :], 0.0),
                     reads=("S_sb",), writes=("S_sb",))
                p.op("gpsimd", lambda g: g.memset(S_bf[:, :], 0.0),
                     reads=("S_bf",), writes=("S_bf",))
            for j in range(CPB):
                ch = B * CPB + j
                jp = j % 2
                p.op("sync", lambda s, j=j, jp=jp: s.dma_start(
                    kb_tok[:, jp * C:(jp + 1) * C], kbar_d[:, j * 128:(j + 1) * 128],
                    transpose=True),
                    reads=("kbar_dram",), writes=(f"kbt{jp}",), inc=16)
                def apos(h_):
                    return (h_ // 8) * 1024 + (h_ % 2) * 512 + ((h_ % 8) // 2) * 128
                for w_ in range(2):
                    def fnA(t, w_=w_, j=j):
                        inst = None
                        for par in range(2):
                            for idx in range(4):
                                h_ = 8 * w_ + 2 * idx + par
                                o = h_ // 2
                                rb = 64 * par
                                inst = t.matmul(
                                    pbA[par][:, idx * 128:(idx + 1) * 128],
                                    k_fm[rb:rb + 64, o * TB + j * L: o * TB + (j + 1) * L],
                                    r_fm[rb:rb + 64, o * TB + j * L: o * TB + (j + 1) * L],
                                    start=True, stop=True,
                                    tile_position=(rb, 0))
                        return inst
                    p.op("tensor", fnA, reads=("k_fm", "r_fm"),
                         writes=("pbA0", "pbA1"))
                    for par in range(2):
                        p.op("vector", lambda v, w_=w_, par=par: v.tensor_mul(
                            abuf[:, w_ * 1024 + par * 512: w_ * 1024 + (par + 1) * 512]
                            .rearrange("q2 (a b) -> q2 a b", b=128),
                            pbA[par].ap().rearrange("q2 (a b) -> q2 a b", b=128),
                            trilmask.ap().unsqueeze(1).to_broadcast([128, 4, 128])),
                            reads=(f"pbA{par}", "trilmask"), writes=("abuf",))

                def fnD(t, j=j):
                    inst = None
                    for g_ in range(8):
                        inst = t.matmul(
                            pbsm[:, 2 * g_:2 * g_ + 2],
                            ruk_fm[:, g_ * TB + j * L: g_ * TB + (j + 1) * L],
                            ublk[:, 2 * g_:2 * g_ + 2],
                            start=True, stop=True)
                    return inst
                p.op("tensor", fnD, reads=("ruk_fm", "ublk"), writes=("pbsm",))
                p.op("vector", lambda v: v.tensor_copy(d_sb[:, :], pbsm[:, 0:H]),
                     reads=("pbsm",), writes=("d_sb",))

                for par in range(2):
                    pY = pbY[par]

                    def fnY(t, par=par, j=j, pY=pY):
                        inst = None
                        rb = 64 * par
                        for i8 in range(8):
                            h_ = 2 * i8 + par
                            o = i8
                            t.matmul(
                                pY[:, i8 * 64:(i8 + 1) * 64],
                                r_fm[rb:rb + 64, o * TB + j * L: o * TB + (j + 1) * L],
                                S_bf[rb:rb + 64, o * 64:(o + 1) * 64],
                                start=True, stop=False,
                                tile_position=(rb, 0))
                            inst = t.matmul(
                                pY[:, i8 * 64:(i8 + 1) * 64],
                                abuf[:, apos(h_): apos(h_) + 128],
                                v_tok[:, j * C + h_ * 64: j * C + (h_ + 1) * 64],
                                start=False, stop=True)
                        return inst
                    p.op("tensor", fnY, reads=("abuf", "r_fm", "v_tok", "S_bf"),
                         writes=(f"pbY{par}",))

                def fnS(t, j=j, jp=jp):
                    inst = None
                    for g_ in range(8):
                        for sub in range(2):
                            h_ = 2 * g_ + sub
                            inst = t.matmul(
                                pbS[sub * 64:(sub + 1) * 64, g_ * 64:(g_ + 1) * 64],
                                kb_tok[:, jp * C + h_ * 64:jp * C + (h_ + 1) * 64],
                                v_tok[:, j * C + h_ * 64:j * C + (h_ + 1) * 64],
                                start=True, stop=True,
                                tile_position=(0, sub * 64))
                    return inst
                p.op("tensor", fnS, reads=(f"kbt{jp}", "v_tok"), writes=("pbS",))
                p.op("gpsimd", lambda g, j=j: g.tensor_mul(
                    Sdec.ap().rearrange("q (a b) -> q a b", b=HD),
                    S_sb.ap().rearrange("q (a b) -> q a b", b=HD),
                    dend.ap().rearrange("q (a b) -> q b a", b=CPB)[:, j, :]
                    .unsqueeze(2).to_broadcast([128, KC, HD])),
                    reads=("S_sb", "dend"), writes=("Sdec",))
                p.op("vector", lambda v: v.tensor_add(S_sb[:, :], pbS[:, :], Sdec[:, :]),
                     reads=("pbS", "Sdec"), writes=("S_sb",))
                p.op("vector", lambda v: v.tensor_copy(S_bf[:, :], S_sb[:, :]),
                     reads=("S_sb",), writes=("S_bf",))

                p.op("vector", lambda v, j=j: v.tensor_mul(
                    dv_t.ap().rearrange("q (a b) -> q a b", b=HD),
                    v_tok[:, j * C:(j + 1) * C].rearrange("q (a b) -> q a b", b=HD),
                    d_sb.ap().unsqueeze(2).to_broadcast([128, H, HD])),
                    reads=("v_tok", "d_sb"), writes=("dv_t",))
                for par in range(2):
                    p.op("vector", lambda v, par=par: v.tensor_add(
                        y_sb.ap().rearrange("q (a p d) -> q a p d", p=2, d=HD)[:, :, par, :],
                        pbY[par].ap().rearrange("q (a d) -> q a d", d=HD),
                        dv_t.ap().rearrange("q (a p d) -> q a p d", p=2, d=HD)[:, :, par, :]),
                        reads=(f"pbY{par}", "dv_t"), writes=("y_sb",))

                p.op("vector", lambda v: v.tensor_reduce(
                    gn16a[:, :], y_sb.ap().rearrange("q (h d) -> q h d", h=H),
                    AX.X, OP.add), reads=("y_sb",), writes=("gn16a",))
                p.op("scalar", lambda s: s.activation(
                    sqscr[:, :], y_sb[:, 0:512], AF.Square),
                    reads=("y_sb",), writes=("sqscr",))
                p.op("vector", lambda v: v.tensor_reduce(
                    gn16b[:, 0:8], sqscr.ap().rearrange("q (h d) -> q h d", h=8),
                    AX.X, OP.add), reads=("sqscr",), writes=("gn16b",))
                p.op("scalar", lambda s: s.activation(
                    sqscr[:, :], y_sb[:, 512:1024], AF.Square),
                    reads=("y_sb", "gn16b"), writes=("sqscr",))
                p.op("vector", lambda v: v.tensor_reduce(
                    gn16b[:, 8:16], sqscr.ap().rearrange("q (h d) -> q h d", h=8),
                    AX.X, OP.add), reads=("sqscr",), writes=("gn16b",))
                p.op("vector", lambda v: v.tensor_scalar(
                    gn16a[:, :], gn16a[:, :], 1.0 / HD, None, OP.mult),
                    reads=("gn16a",), writes=("gn16a",))
                p.op("vector", lambda v: v.tensor_mul(
                    gnal[:, :], gn16a[:, :], gn16a[:, :]),
                    reads=("gn16a",), writes=("gnal",))
                p.op("vector", lambda v: v.scalar_tensor_tensor(
                    gnal[:, :], gn16b[:, :], 1.0 / HD, gnal[:, :],
                    OP.mult, OP.subtract),
                    reads=("gn16b", "gnal"), writes=("gnal",))
                p.op("scalar", lambda s: s.activation(
                    gnal[:, :], gnal[:, :], AF.Sqrt, bias=GN_EPS, scale=1.0),
                    reads=("gnal", "consts"), writes=("gnal",))
                p.op("vector", lambda v: v.reciprocal(gnal[:, :], gnal[:, :]),
                     reads=("gnal",), writes=("gnal",))
                p.op("vector", lambda v: v.tensor_mul(
                    gnbe[:, :], gn16a[:, :], gnal[:, :]),
                    reads=("gn16a", "gnal"), writes=("gnbe",))
                p.op("vector", lambda v: v.tensor_mul(
                    yn_tok.ap().rearrange("q (h d) -> q h d", h=H),
                    y_sb.ap().rearrange("q (h d) -> q h d", h=H),
                    gnal.ap().unsqueeze(2).to_broadcast([128, H, HD])),
                    reads=("y_sb", "gnal"), writes=("yn_tok",))
                p.op("vector", lambda v: v.tensor_sub(
                    yn_tok.ap().rearrange("q (h d) -> q h d", h=H),
                    yn_tok.ap().rearrange("q (h d) -> q h d", h=H),
                    gnbe.ap().unsqueeze(2).to_broadcast([128, H, HD])),
                    reads=("yn_tok", "gnbe"), writes=("yn_tok",))
                rows_o = yn_d.rearrange("(a q) c -> a q c", q=128)[ch]
                p.op("sync", lambda s, rows_o=rows_o: s.dma_start(rows_o, yn_tok[:, :]),
                     reads=("yn_tok",), writes=("yn_dram",), inc=16)

            for k in range(KC):
                src = yn_d[tok0:tok0 + TB, k * 128:(k + 1) * 128]
                p.op("sync", lambda s, src=src, k=k: s.dma_start(
                    ynfm[:, k * TB:(k + 1) * TB], src, transpose=True),
                    reads=("yn_dram", "ynfm"), writes=("ynfm",), inc=16)

            for j in range(CPB):
                ch = B * CPB + j
                for half in range(2):
                    pt = pbb[half]

                    def fnO(t, j=j, half=half, pt=pt):
                        for k in range(KC):
                            t.matmul(
                                pt[:, :],
                                ynfm[:, k * TB + j * 128: k * TB + (j + 1) * 128],
                                w_o[:, k * C + half * 512: k * C + (half + 1) * 512],
                                start=(k == 0), stop=False)
                        return t.matmul(pt[:, :], ones1b[0:1, :],
                                        btil[0:1, half * 512:(half + 1) * 512],
                                        start=False, stop=True)
                    p.op("tensor", fnO, reads=("ynfm", "w_o", "btil", "ones1b"),
                         writes=(f"pb{half}",))
                    p.op("scalar", lambda s, half=half, pt=pt: s.activation(
                        ao_sb[:, half * 512:(half + 1) * 512], pt[:, :], AF.Copy,
                        accum_out=aoacc[:, half:half + 1]),
                        reads=(f"pb{half}",), writes=("ao_sb", "aoacc"))
                for half in range(2):
                    p.op("scalar", lambda s, half=half: s.activation(
                        sqscr[:, :], ao_sb[:, half * 512:(half + 1) * 512], AF.Square,
                        accum_out=aoacc[:, 2 + half:3 + half]),
                        reads=("ao_sb",), writes=("sqscr", "aoacc"))
                p.op("vector", lambda v: v.tensor_add(
                    mcol[:, :], aoacc[:, 0:1], aoacc[:, 1:2]),
                    reads=("aoacc",), writes=("mcol",))
                p.op("vector", lambda v: v.tensor_scalar(
                    mcol[:, :], mcol[:, :], 1.0 / C, None, OP.mult),
                    reads=("mcol",), writes=("mcol",))
                p.op("vector", lambda v: v.tensor_add(
                    rstdcol[:, :], aoacc[:, 2:3], aoacc[:, 3:4]),
                    reads=("aoacc",), writes=("rstdcol",))
                p.op("vector", lambda v: v.tensor_mul(
                    tcol[:, :], mcol[:, :], mcol[:, :]),
                    reads=("mcol",), writes=("tcol",))
                p.op("vector", lambda v: v.scalar_tensor_tensor(
                    rstdcol[:, :], rstdcol[:, :], 1.0 / C, tcol[:, :],
                    OP.mult, OP.subtract),
                    reads=("rstdcol", "tcol"), writes=("rstdcol",))
                p.op("scalar", lambda s: s.activation(
                    rstdcol[:, :], rstdcol[:, :], AF.Sqrt, bias=LN_EPS, scale=1.0),
                    reads=("rstdcol", "consts"), writes=("rstdcol",))
                p.op("vector", lambda v: v.reciprocal(rstdcol[:, :], rstdcol[:, :]),
                     reads=("rstdcol",), writes=("rstdcol",))
                rows = x_in.rearrange("(a q) c -> a q c", q=128)[ch]
                p.op("scalar", lambda s, rows=rows: s.dma_start(x2_sb[:, :], rows),
                     writes=("x2_sb",), inc=16)
                p.op("vector", lambda v: v.tensor_scalar(
                    ao_sb[:, :], ao_sb[:, :], mcol[:, :], rstdcol[:, :],
                    OP.subtract, OP.mult),
                    reads=("ao_sb", "mcol", "rstdcol"), writes=("ao_sb",))
                p.op("vector", lambda v: v.tensor_add(
                    x2_sb[:, :], x2_sb[:, :], ao_sb[:, :]),
                    reads=("x2_sb", "ao_sb"), writes=("x2_sb",))
                rows_o = x2_d.rearrange("(a q) c -> a q c", q=128)[ch]
                p.op("sync", lambda s, rows_o=rows_o: s.dma_start(rows_o, x2_sb[:, :]),
                     reads=("x2_sb",), writes=("x2_dram",), inc=16)
                p.op("vector", lambda v: v.bn_stats(bn6[:, 0:6], x2_sb[:, 0:512]),
                     reads=("x2_sb",), writes=("bn6",))
                p.op("vector", lambda v: v.bn_stats(bn6[:, 6:12], x2_sb[:, 512:1024]),
                     reads=("x2_sb",), writes=("bn6",))
                p.op("vector", lambda v: v.bn_aggr(bn2[:, :], bn6[:, 0:12]),
                     reads=("bn6",), writes=("bn2",))
                p.op("scalar", lambda s: s.activation(
                    tcol[:, :], bn2[:, 1:2], AF.Sqrt, bias=LN_EPS, scale=1.0),
                    reads=("bn2", "consts"), writes=("tcol",))
                p.op("vector", lambda v: v.reciprocal(tcol[:, :], tcol[:, :]),
                     reads=("tcol",), writes=("tcol",))
                p.op("vector", lambda v: v.tensor_scalar(
                    xn_tok[:, :], x2_sb[:, :], bn2[:, 0:1], tcol[:, :],
                    OP.subtract, OP.mult),
                    reads=("x2_sb", "bn2", "tcol"), writes=("xln_tok",))
                rows_o = xn_d.rearrange("(a q) c -> a q c", q=128)[ch]
                p.op("sync", lambda s, rows_o=rows_o: s.dma_start(rows_o, xn_tok[:, :]),
                     reads=("xln_tok",), writes=("xn_dram",), inc=16)

        if not do_ffn:
            for ch in range(n_blocks * CPB):
                rows_i = x2_d.rearrange("(a q) c -> a q c", q=128)[ch]
                rows_o = y_out.rearrange("(a q) c -> a q c", q=128)[ch]
                p.op("scalar", lambda s, rows_i=rows_i: s.dma_start(fin_t[:, :], rows_i),
                     reads=("x2_dram", "y_sb"), writes=("y_sb",), inc=16)
                p.op("sync", lambda s, rows_o=rows_o: s.dma_start(rows_o, fin_t[:, :]),
                     reads=("y_sb",), writes=("y_out",), inc=16)
            p.emit(block, handles)
            return nc, dram

        p.op("gpsimd", lambda g: g.collective_compute(
            "AllGather", OP.bypass, replica_groups=[list(range(NCORE))],
            ins=[xn_d], outs=[xn_all[0:TOT, :]]),
            reads=("xn_dram",), writes=("xn_all",), sem_as="ag1", inc=1)

        src_sgu = dram["wsgu"].rearrange("(a b) c -> b a c", b=128)
        p.op("gpsimd", lambda g: g.dma_start(
            w_sgu_a.ap().rearrange("q (a c) -> q a c", c=C), src_sgu),
            reads=("w_o",), writes=("w_o",), inc=16)
        src_sd = dram["wsd"].rearrange("(a b) c -> b a c", b=128)
        p.op("gpsimd", lambda g: g.dma_start(
            w_sd_a.ap()[:, 0:4 * C].rearrange("q (a c) -> q a c", c=C), src_sd),
            reads=("xlnfm",), writes=("xlnfm",), inc=16)

        for B in range(n_blocks):
            tok0 = B * TB
            for k in range(KC):
                src = xn_d[tok0:tok0 + TB, k * 128:(k + 1) * 128]
                p.op("sync", lambda s, src=src, k=k: s.dma_start(
                    ynfm[:, k * TB:(k + 1) * TB], src, transpose=True),
                    reads=("xn_dram", "ynfm"), writes=("ynfm",), inc=16)
            for j in range(CPB):
                ch = B * CPB + j

                def fnR(t, j=j):
                    inst = None
                    for k in range(KC):
                        inst = t.matmul(
                            pbsm[:, 32:64],
                            ynfm[:, k * TB + j * 128: k * TB + (j + 1) * 128],
                            gwT[:, k * E:(k + 1) * E],
                            start=(k == 0), stop=(k == KC - 1))
                    return inst
                p.op("tensor", fnR, reads=("ynfm", "gwT"), writes=("pbsm",))
                p.op("vector", lambda v, ch=ch: v.tensor_copy(
                    logits_sb[:, ch * E:(ch + 1) * E], pbsm[:, 32:64]),
                    reads=("pbsm", "r_fm"), writes=("r_fm",))
            for k8 in range(KC):
                pt = pbb[k8 % 2]

                def fnG(t, k8=k8, pt=pt):
                    inst = None
                    for k in range(KC):
                        inst = t.matmul(
                            pt[:, :],
                            w_sgu_a[:, k * C + k8 * 128: k * C + k8 * 128 + 128],
                            ynfm[:, k * TB:(k + 1) * TB],
                            start=(k == 0), stop=(k == KC - 1))
                    return inst
                p.op("tensor", fnG, reads=("ynfm", "w_o"), writes=(f"pb{k8 % 2}",))
                if k8 < 4:
                    p.op("scalar", lambda s, k8=k8, pt=pt: s.activation(
                        h_fm[:, k8 * TB:(k8 + 1) * TB], pt[:, :], AF.Silu),
                        reads=(f"pb{k8 % 2}",), writes=("xxfm",))
                else:
                    p.op("vector", lambda v, k8=k8, pt=pt: v.tensor_mul(
                        h_fm[:, (k8 - 4) * TB:(k8 - 3) * TB],
                        h_fm[:, (k8 - 4) * TB:(k8 - 3) * TB], pt[:, :]),
                        reads=(f"pb{k8 % 2}", "xxfm"), writes=("xxfm",))
            for j in range(CPB):
                ch = B * CPB + j
                rt = rout_t[j % 2]
                rtk = f"rout{j % 2}"
                for half in range(2):
                    pt = pbb[half]

                    def fnD2(t, j=j, half=half, pt=pt):
                        inst = None
                        for k in range(4):
                            inst = t.matmul(
                                pt[:, :],
                                h_fm[:, k * TB + j * 128: k * TB + (j + 1) * 128],
                                w_sd_a[:, k * C + half * 512: k * C + (half + 1) * 512],
                                start=(k == 0), stop=(k == 3))
                        return inst
                    p.op("tensor", fnD2, reads=("xxfm", "xlnfm"), writes=(f"pb{half}",))
                    p.op("scalar", lambda s, half=half, pt=pt, rt=rt: s.activation(
                        rt[:, half * 512:(half + 1) * 512], pt[:, :], AF.Copy),
                        reads=(f"pb{half}",), writes=(rtk,))
                rows_o = sh_d.rearrange("(a q) c -> a q c", q=128)[ch]
                p.op("sync", lambda s, rows_o=rows_o, rt=rt: s.dma_start(rows_o, rt[:, :]),
                     reads=(rtk, "xf"), writes=("sh_dram",), inc=16)

        for ch in range(NCH):
            p.op("vector", lambda v, ch=ch: v.max(
                out=m8_sb[:, ch * 8:(ch + 1) * 8],
                in_=logits_sb[:, ch * E:(ch + 1) * E]),
                reads=("r_fm",), writes=("r_fm",))
        for ch in range(NCH):
            p.op("vector", lambda v, ch=ch: v.tensor_copy(
                thrT[:, ch:ch + 1], m8_sb[:, ch * 8 + 3:ch * 8 + 4]),
                reads=("r_fm",), writes=("r_fm",))
            p.op("vector", lambda v, ch=ch: v.tensor_copy(
                m1T[:, ch:ch + 1], m8_sb[:, ch * 8:ch * 8 + 1]),
                reads=("r_fm",), writes=("r_fm",))
        p.op("vector", lambda v: v.tensor_tensor(
            maskb.rearrange("q (a e) -> q a e", e=E),
            logits_sb.rearrange("q (a e) -> q a e", e=E),
            thrT.unsqueeze(2).to_broadcast([128, NCH, E]), OP.is_ge),
            reads=("r_fm", "v_tok"), writes=("v_tok",))
        p.op("vector", lambda v: v.tensor_tensor(
            gates_sb.rearrange("q (a e) -> q a e", e=E),
            logits_sb.rearrange("q (a e) -> q a e", e=E),
            m1T.unsqueeze(2).to_broadcast([128, NCH, E]), OP.subtract),
            reads=("r_fm", "k_fm"), writes=("k_fm",))
        p.op("scalar", lambda s: s.activation(gates_sb, gates_sb, AF.Exp),
             reads=("k_fm",), writes=("k_fm",))
        p.op("vector", lambda v: v.tensor_mul(gates_sb, gates_sb, maskb),
             reads=("k_fm", "v_tok"), writes=("k_fm",))
        p.op("vector", lambda v: v.tensor_reduce(
            thrT, gates_sb.rearrange("q (a e) -> q a e", e=E), AX.X, OP.add),
            reads=("k_fm", "r_fm"), writes=("r_fm",))
        p.op("vector", lambda v: v.reciprocal(thrT, thrT),
             reads=("r_fm",), writes=("r_fm",))
        p.op("vector", lambda v: v.tensor_mul(
            gates_sb.rearrange("q (a e) -> q a e", e=E),
            gates_sb.rearrange("q (a e) -> q a e", e=E),
            thrT.unsqueeze(2).to_broadcast([128, NCH, E])),
            reads=("k_fm", "r_fm"), writes=("k_fm",))

        cnts = x2_sb[0:1, 0:NCH * E]
        for ch in range(NCH):
            p.op("vector", lambda v, ch=ch: v.tensor_copy(
                maskf[:, :], maskb[:, ch * E:(ch + 1) * E]),
                reads=("v_tok",), writes=("maskf",))
            p.op("tensor", lambda t, ch=ch: t.matmul(
                pbsm[0:1, 0:E], onescol_f[:, :], maskf[:, :],
                start=True, stop=True),
                reads=("maskf", "onescol_f"), writes=("pbsm",))
            p.op("vector", lambda v, ch=ch: v.tensor_copy(
                cnts[:, ch * E:(ch + 1) * E], pbsm[0:1, 0:E]),
                reads=("pbsm",), writes=("x2_sb",))
        p.op("vector", lambda v: v.tensor_reduce(
            cntrow[:, :], cnts.rearrange("q (a b) -> q b a", b=E), AX.X, OP.add),
            reads=("x2_sb",), writes=("cntrow",))
        p.op("sync", lambda s: s.dma_start(cnt_d, cntrow[:, :]),
             reads=("cntrow",), writes=("cnt_dram",), inc=16)
        p.op("gpsimd", lambda g: g.collective_compute(
            "AllGather", OP.bypass, replica_groups=[list(range(NCORE))],
            ins=[cnt_d], outs=[cnt8_d]),
            reads=("cnt_dram",), writes=("cnt8_dram",), sem_as="ag0", inc=1)
        p.op("sync", lambda s: s.dma_start(cnt8_sb[:, :], cnt8_d),
             reads=("cnt8_dram",), writes=("cnt8_sb",), inc=16)
        p.op("tensor", lambda t: t.matmul(
            pbsm[0:1, 64:64 + E], ind_sb[:, :], cnt8_sb[:, :], start=True, stop=True),
            reads=("ind_sb", "cnt8_sb"), writes=("pbsm",))
        p.op("vector", lambda v: v.tensor_copy(
            chpfx[:, 0:E], pbsm[0:1, 64:64 + E]),
            reads=("pbsm",), writes=("chpfx",))
        for ch in range(1, NCH):
            p.op("vector", lambda v, ch=ch: v.tensor_add(
                chpfx[:, ch * E:(ch + 1) * E], chpfx[:, (ch - 1) * E:ch * E],
                cnts[:, (ch - 1) * E:ch * E]),
                reads=("chpfx", "x2_sb"), writes=("chpfx",))
        for ch in range(NCH):
            p.op("vector", lambda v, ch=ch: v.tensor_copy(
                maskf[:, :], maskb[:, ch * E:(ch + 1) * E]),
                reads=("v_tok",), writes=("maskf",))

            def fnPos(t, ch=ch):
                t.matmul(pbsm[:, 96:96 + E], stril_f[:, :], maskf[:, :],
                         start=True, stop=False)
                return t.matmul(pbsm[:, 96:96 + E], ones1f[0:1, :],
                                chpfx[0:1, ch * E:(ch + 1) * E],
                                start=False, stop=True)
            p.op("tensor", fnPos, reads=("maskf", "stril_f", "ones1f", "chpfx"),
                 writes=("pbsm",))
            p.op("vector", lambda v, ch=ch: v.tensor_add(
                posg_sb[:, ch * E:(ch + 1) * E], pbsm[:, 96:96 + E], iotak[:, :]),
                reads=("pbsm", "iotak", "k_fm"), writes=("k_fm",))

        p.op("vector", lambda v: v.tensor_tensor(
            keep_sb.rearrange("q (a e) -> q a e", e=E),
            posg_sb.rearrange("q (a e) -> q a e", e=E),
            iotak1024.ap().unsqueeze(1).to_broadcast([128, NCH, E]), OP.is_lt),
            reads=("k_fm", "iotak1024", "ruk_fm"), writes=("ruk_fm",))
        p.op("vector", lambda v: v.tensor_mul(keep_sb, keep_sb, maskb),
             reads=("ruk_fm", "v_tok"), writes=("ruk_fm",))
        p.op("vector", lambda v: v.scalar_tensor_tensor(
            tbl_sb, keep_sb, -BIG, posg_sb, OP.mult, OP.add),
            reads=("ruk_fm", "k_fm"), writes=("ruk_fm",))
        p.op("vector", lambda v: v.tensor_scalar(
            tbl_sb, tbl_sb, BIG, None, OP.add),
            reads=("ruk_fm",), writes=("ruk_fm",))
        p.op("vector", lambda v: v.tensor_mul(gates_sb, gates_sb, keep_sb),
             reads=("k_fm", "ruk_fm"), writes=("k_fm",))

        for ch in range(NCH):
            for kk in range(4):
                idx = ch * 4 + kk
                p.op("vector", lambda v, ch=ch, kk=kk, idx=idx: v.scalar_tensor_tensor(
                    sqscr[:, 0:E], logits_sb[:, ch * E:(ch + 1) * E],
                    m8_sb[:, ch * 8 + kk:ch * 8 + kk + 1],
                    tbl_sb[:, ch * E:(ch + 1) * E],
                    OP.is_equal, OP.mult,
                    accum_out=slotcols[:, idx:idx + 1]),
                    reads=("r_fm", "ruk_fm", "v_tok"), writes=("sqscr", "v_tok"))
                p.op("vector", lambda v, ch=ch, kk=kk, idx=idx: v.scalar_tensor_tensor(
                    sqscr[:, 0:E], logits_sb[:, ch * E:(ch + 1) * E],
                    m8_sb[:, ch * 8 + kk:ch * 8 + kk + 1],
                    gates_sb[:, ch * E:(ch + 1) * E],
                    OP.is_equal, OP.mult,
                    accum_out=gcols[:, idx:idx + 1]),
                    reads=("r_fm", "k_fm", "v_tok"), writes=("sqscr", "v_tok"))
        p.op("vector", lambda v: v.tensor_tensor(
            slotcols, slotcols, trash_iota[:, :], OP.min),
            reads=("v_tok", "trash_iota"), writes=("v_tok",))
        p.op("vector", lambda v: v.tensor_copy(slotints, slotcols),
             reads=("v_tok",), writes=("v_tok",))
        p.op("vector", lambda v: v.tensor_copy(
            sc_all.rearrange("q (a b t) -> q a b t", b=4, t=2)[:, :, :, 0],
            iota_tg1.ap().unsqueeze(2).to_broadcast([128, NCH, 4])),
            reads=("iota_tg1", "v_tok"), writes=("v_tok",))
        p.op("vector", lambda v: v.tensor_copy(
            sc_all.rearrange("q (a b t) -> q a b t", b=4, t=2)[:, :, :, 1],
            gcols.rearrange("q (a b) -> q a b", b=4)),
            reads=("v_tok",), writes=("v_tok",))
        for ch in range(NCH):
            for kk in range(4):
                idx = ch * 4 + kk
                p.op("gpsimd", lambda g, idx=idx, kk=kk: g.indirect_dma_start(
                    slotmap4_d[kk], bass.IndirectOffsetOnAxis(slotints[:, idx:idx + 1], 0),
                    sc_all[:, 2 * idx:2 * idx + 2], None),
                    reads=("v_tok", f"smz{kk}", f"smw{kk}"), writes=(f"smw{kk}",), inc=16)
        for kk in range(4):
            p.op("gpsimd", lambda g, kk=kk: g.collective_compute(
                "AllReduce", OP.add, replica_groups=[list(range(NCORE))],
                ins=[slotmap4_d[kk][0:E * CAP, :]], outs=[slotmapG4_d[kk]]),
                reads=(f"smw{kk}",), writes=("slotmapG",), sem_as="ar1", inc=1)

        for q in range(EPC):
            p.op("vector", lambda v, q=q: v.tensor_scalar(
                capf[:, q * 8:(q + 1) * 8], iotasch[:, :],
                meta_sb[:, 1 + q:2 + q], None, OP.add),
                reads=("iotasch", "meta_sb"), writes=("capf",))
        p.op("vector", lambda v: v.tensor_copy(capidx_all[:, :], capf[:, :]),
             reads=("capf",), writes=("capidx_all",))
        for i in range(EPC * 8):
            for kk in range(4):
                p.op("gpsimd", lambda g, i=i, kk=kk: g.indirect_dma_start(
                    smrows_all[:, 8 * i + 2 * kk:8 * i + 2 * kk + 2], None,
                    slotmapG4_d[kk], bass.IndirectOffsetOnAxis(capidx_all[:, i:i + 1], 0)),
                    reads=("capidx_all", "slotmapG"), writes=(f"smr{i}",), inc=16)
        smr_keys = tuple(f"smr{i}" for i in range(EPC * 8))
        sm4 = smrows_all.ap().rearrange("q (a kk t) -> q a t kk", kk=4, t=2)
        p.op("vector", lambda v: v.tensor_reduce(
            capf2_all.ap().rearrange("q (a t) -> q a t", t=2), sm4, AX.X, OP.add),
            reads=smr_keys, writes=("capf2_all",))
        sm2 = capf2_all.ap().rearrange("q (a t) -> q a t", t=2)
        p.op("vector", lambda v: v.tensor_scalar(
            capf[:, :], sm2[:, :, 0], 1.0, None, OP.subtract),
            reads=("capf2_all", "capidx_all"), writes=("capf",))
        p.op("vector", lambda v: v.tensor_copy(smg_all[:, :], sm2[:, :, 1]),
             reads=("capf2_all",), writes=("smg_all",))
        p.op("vector", lambda v: v.tensor_scalar(
            sqscr[:, 0:EPC * 8], capf[:, :], 0.0, None, OP.is_lt),
            reads=("capf",), writes=("sqscr",))
        p.op("vector", lambda v: v.scalar_tensor_tensor(
            capf[:, :], sqscr[:, 0:EPC * 8], float(TOT + 50), capf[:, :],
            OP.mult, OP.add),
            reads=("sqscr", "capf"), writes=("capf",))
        p.op("vector", lambda v: v.tensor_scalar(
            capf[:, :], capf[:, :], float(TOT), None, OP.min),
            reads=("capf",), writes=("capf",))
        p.op("vector", lambda v: v.tensor_copy(capidx_all[:, :], capf[:, :]),
             reads=("capf",), writes=("capidx_all",))
        for i in range(EPC * 8):
            bb_ = rout_t[i % NROUT]
            bbk = f"rout{i % NROUT}"
            p.op("gpsimd", lambda g, i=i, bb_=bb_: g.indirect_dma_start(
                bb_[:, :], None,
                xn_all, bass.IndirectOffsetOnAxis(capidx_all[:, i:i + 1], 0)),
                reads=("capidx_all", "xn_all", "xnall_z", bbk, "xf"),
                writes=(bbk,), inc=16)
            p.op("sync", lambda s, i=i, bb_=bb_: s.dma_start(
                cap_d[i * 128:(i + 1) * 128, :], bb_[:, :]),
                reads=(bbk, "xf"), writes=(f"capd{i}",), inc=16)

        for q in range(EPC):
            src_gu = dram["wgu_own"].rearrange("e (a b) c -> e b a c", b=128)[q]
            p.op("gpsimd", lambda g, src_gu=src_gu: g.dma_start(
                w_gu_e.ap().rearrange("q (a c) -> q a c", c=C), src_gu),
                reads=("w_r",), writes=("w_r",), inc=16)
            src_d = dram["wd_own"].rearrange("e (a b) c -> e b a c", b=128)[q]
            p.op("gpsimd", lambda g, src_d=src_d: g.dma_start(
                w_d_e[:, 0:4 * C].rearrange("q (a c) -> q a c", c=C), src_d),
                reads=("w_k",), writes=("w_k",), inc=16)
            for k in range(KC):
                p.op("sync", lambda s, q=q, k=k: s.dma_start(
                    capfm[:, k * CAP:(k + 1) * CAP],
                    cap_d[q * CAP:(q + 1) * CAP, k * 128:(k + 1) * 128], transpose=True),
                    reads=tuple(f"capd{q * 8 + sch}" for sch in range(8)) + ("w_v",),
                    writes=("w_v",), inc=16)
            for k8 in range(KC):
                for half in range(2):
                    pt = pbb[half]

                    def fnEG(t, k8=k8, half=half, pt=pt):
                        inst = None
                        for k in range(KC):
                            inst = t.matmul(
                                pt[:, :],
                                w_gu_e[:, k * C + k8 * 128: k * C + k8 * 128 + 128],
                                capfm[:, k * CAP + half * 512: k * CAP + (half + 1) * 512],
                                start=(k == 0), stop=(k == KC - 1))
                        return inst
                    p.op("tensor", fnEG, reads=("w_r", "w_v"), writes=(f"pb{half}",))
                    if k8 < 4:
                        p.op("scalar", lambda s, k8=k8, half=half, pt=pt: s.activation(
                            h_fm[:, k8 * CAP + half * 512: k8 * CAP + (half + 1) * 512],
                            pt[:, :], AF.Silu),
                            reads=(f"pb{half}",), writes=("xxfm",))
                    else:
                        p.op("vector", lambda v, k8=k8, half=half, pt=pt: v.tensor_mul(
                            h_fm[:, (k8 - 4) * CAP + half * 512:
                                 (k8 - 4) * CAP + (half + 1) * 512],
                            h_fm[:, (k8 - 4) * CAP + half * 512:
                                 (k8 - 4) * CAP + (half + 1) * 512],
                            pt[:, :]),
                            reads=(f"pb{half}", "xxfm"), writes=("xxfm",))
            for sch in range(8):
                rt = rout_t[sch % 2]
                rtk = f"rout{sch % 2}"
                for half in range(2):
                    pt = pbb[half]

                    def fnED(t, sch=sch, half=half, pt=pt):
                        inst = None
                        for k in range(4):
                            inst = t.matmul(
                                pt[:, :],
                                h_fm[:, k * CAP + sch * 128: k * CAP + (sch + 1) * 128],
                                w_d_e[:, k * C + half * 512: k * C + (half + 1) * 512],
                                start=(k == 0), stop=(k == 3))
                        return inst
                    p.op("tensor", fnED, reads=("xxfm", "w_k"), writes=(f"pb{half}",))
                    p.op("scalar", lambda s, q=q, sch=sch, half=half, pt=pt, rt=rt: s.mul(
                        rt[:, half * 512:(half + 1) * 512], pt[:, :],
                        smg_all[:, q * 8 + sch:q * 8 + sch + 1]),
                        reads=(f"pb{half}", "smg_all"), writes=(rtk,))
                rows_o = eout_d.rearrange("(a q) c -> a q c", q=128)[q * 8 + sch]
                p.op("sync", lambda s, rows_o=rows_o, rt=rt: s.dma_start(rows_o, rt[:, :]),
                     reads=(rtk, "xf"), writes=("eout_dram",), inc=16)

        p.op("gpsimd", lambda g: g.collective_compute(
            "AllGather", OP.bypass, replica_groups=[list(range(NCORE))],
            ins=[eout_d], outs=[eoutG_d[0:E * CAP, :]]),
            reads=("eout_dram",), writes=("eoutG",), sem_as="ag2", inc=1)

        for w0 in range(0, NCH, NROUT):
            wave = list(range(w0, min(w0 + NROUT, NCH)))
            for ch in wave:
                rt = rout_t[ch % NROUT]
                rtk = f"rout{ch % NROUT}"
                rows_i = sh_d.rearrange("(a q) c -> a q c", q=128)[ch]
                p.op("scalar", lambda s, rows_i=rows_i, rt=rt: s.dma_start(rt[:, :], rows_i),
                     reads=("sh_dram", rtk, "xf"), writes=(rtk,), inc=16)
            for kk in range(4):
                for ch in wave:
                    rt = rout_t[ch % NROUT]
                    rtk = f"rout{ch % NROUT}"
                    idx = ch * 4 + kk
                    p.op("gpsimd", lambda g, idx=idx, rt=rt: g.indirect_dma_start(
                        rt[:, :], None,
                        eoutG_d, bass.IndirectOffsetOnAxis(slotints[:, idx:idx + 1], 0),
                        compute_op=OP.add),
                        reads=("v_tok", "eoutG", "eoutG_z", rtk), writes=(rtk,),
                        inc=16)
            for ch in wave:
                rt = rout_t[ch % NROUT]
                rtk = f"rout{ch % NROUT}"
                p.op("vector", lambda v, rt=rt: v.bn_stats(bn6[:, 0:6], rt[:, 0:512]),
                     reads=(rtk, "xf"), writes=("bn6",))
                p.op("vector", lambda v, rt=rt: v.bn_stats(bn6[:, 6:12], rt[:, 512:1024]),
                     reads=(rtk, "xf"), writes=("bn6",))
                p.op("vector", lambda v: v.bn_aggr(bn2[:, :], bn6[:, 0:12]),
                     reads=("bn6",), writes=("bn2",))
                p.op("scalar", lambda s: s.activation(
                    tcol[:, :], bn2[:, 1:2], AF.Sqrt, bias=LN_EPS, scale=1.0),
                    reads=("bn2", "consts"), writes=("tcol",))
                p.op("vector", lambda v: v.reciprocal(tcol[:, :], tcol[:, :]),
                     reads=("tcol",), writes=("tcol",))
                rows_x2 = x2_d.rearrange("(a q) c -> a q c", q=128)[ch]
                p.op("scalar", lambda s, rows_x2=rows_x2: s.dma_start(fin_t[:, :], rows_x2),
                     reads=("x2_dram", "y_sb"), writes=("y_sb",), inc=16)
                p.op("vector", lambda v, rt=rt: v.tensor_scalar(
                    x2_sb[:, :], rt[:, :], bn2[:, 0:1], tcol[:, :],
                    OP.subtract, OP.mult),
                    reads=(rtk, "bn2", "tcol", "x2_sb"), writes=("x2_sb",))
                p.op("vector", lambda v: v.tensor_add(
                    fin_t[:, :], fin_t[:, :], x2_sb[:, :]),
                    reads=("y_sb", "x2_sb"), writes=("y_sb",))
                rows_o = y_out.rearrange("(a q) c -> a q c", q=128)[ch]
                p.op("sync", lambda s, rows_o=rows_o: s.dma_start(rows_o, fin_t[:, :]),
                     reads=("y_sb",), writes=("y_out",), inc=16)

        p.emit(block, handles)
    return nc, dram


def make_in_maps(inputs):
    xq = np.ascontiguousarray(np.asarray(inputs["L0_attn_query"], np.float32))
    B_, T_, C_ = xq.shape
    flat = xq.reshape(B_ * T_, C_)

    def core_map(c):
        return {
            "x": np.ascontiguousarray(flat[c * T_CORE:(c + 1) * T_CORE]),
            "maa_x": np.asarray(inputs["L0_attn_time_maa_x"], np.float32),
            "maa_w": np.asarray(inputs["L0_attn_time_maa_w"], np.float32),
            "maa_k": np.asarray(inputs["L0_attn_time_maa_k"], np.float32),
            "maa_v": np.asarray(inputs["L0_attn_time_maa_v"], np.float32),
            "maa_r": np.asarray(inputs["L0_attn_time_maa_r"], np.float32),
            "t_decay": np.asarray(inputs["L0_attn_time_decay"], np.float32),
            "ln_w": np.asarray(inputs["L0_attn_ln_w"], np.float32),
            "ln_b": np.asarray(inputs["L0_attn_ln_b"], np.float32),
            "maa_w1": np.asarray(inputs["L0_attn_time_maa_w1"], np.float32),
            "maa_w2": np.asarray(inputs["L0_attn_time_maa_w2"], np.float32),
            "dw1": np.asarray(inputs["L0_attn_time_decay_w1"], np.float32),
            "dw2": np.asarray(inputs["L0_attn_time_decay_w2"], np.float32),
            "faaaa": np.asarray(inputs["L0_attn_time_faaaa"], np.float32),
            "W_r": np.asarray(inputs["L0_attn_W_r"], np.float32),
            "W_k": np.asarray(inputs["L0_attn_W_k"], np.float32),
            "W_v": np.asarray(inputs["L0_attn_W_v"], np.float32),
            "W_o": np.asarray(inputs["L0_attn_W_o"], np.float32),
            "gate_w": np.asarray(inputs["L0_ffn_gate_w"], np.float32),
            "wgu_own": np.ascontiguousarray(
                np.asarray(inputs["L0_ffn_w_gate_up"], np.float32)[4 * c:4 * c + 4]),
            "wd_own": np.ascontiguousarray(
                np.asarray(inputs["L0_ffn_w_down"], np.float32)[4 * c:4 * c + 4]),
            "wsgu": np.asarray(inputs["L0_ffn_shared_w_gate_up"], np.float32),
            "wsd": np.asarray(inputs["L0_ffn_shared_w_down"], np.float32),
            "meta_ind": (np.arange(8)[:, None] < c).astype(np.float32),
            "meta_cols": np.ascontiguousarray(np.broadcast_to(np.array(
                [c * T_CORE, (4 * c) * CAP, (4 * c + 1) * CAP,
                 (4 * c + 2) * CAP, (4 * c + 3) * CAP, 0, 0, 0],
                np.float32), (128, 8))),
        }
    return [core_map(c) for c in range(NCORE)], (B_, T_, C_)


def kernel(**inputs):
    in_maps, (B_, T_, C_) = make_in_maps(inputs)
    nc, _ = build_nc()
    res = run_bass_kernel_spmd(nc, in_maps, core_ids=list(range(NCORE)))
    out = np.concatenate([res.results[c]["y"] for c in range(NCORE)], axis=0)
    return out.reshape(B_, T_, C_).astype(np.float32)


# version 02
# speedup vs baseline: 1.0595x; 1.0595x over previous
import os
import numpy as np
from contextlib import ExitStack

import concourse.bass as bass
import concourse.mybir as mybir
from concourse.bass_utils import run_bass_kernel_spmd

F32 = mybir.dt.float32
BF16 = mybir.dt.bfloat16
I32 = mybir.dt.int32
AF = mybir.ActivationFunctionType
OP = mybir.AluOpType
AX = mybir.AxisListType

C = 1024
H = 16
HD = 64
KC = 8
NCORE = 8
T_CORE = 4096
SEQ = 2048
TB = 512
NB = 8
L = 128
CPB = 4
NCH = 32
E = 32
EPC = 4
CAP = 1024
I_FF = 512
LN_EPS = 1e-5
GN_EPS = 0.00064
BIG = 65536.0
TOT = NCORE * T_CORE
CC_WAIT = 1

ENGINES = ("sync", "tensor", "vector", "scalar", "gpsimd")


NDQ = 16


class Prog:

    def __init__(self):
        self.ops = {e: [] for e in ENGINES}
        self.count = {e: 0 for e in ENGINES}
        self.dmacnt = {e: 0 for e in ENGINES}
        self.dquses = {}
        self.extra = {}
        self.last_w = {}
        self.readers = {}
        self.waited = {}

    def _need(self, waits, ceng, owner, val):
        if val <= 0:
            return
        k = (ceng, owner)
        if self.waited.get(k, 0) >= val:
            return
        self.waited[k] = val
        waits[owner] = max(waits.get(owner, 0), val)

    nrec = 0
    LIMIT = None

    def op(self, eng, fn, reads=(), writes=(), inc=1, sem_as=None):
        Prog.nrec += 1
        if Prog.LIMIT is not None and Prog.nrec > Prog.LIMIT:
            return
        waits = {}
        for b in reads:
            if b in self.last_w:
                ow, vv = self.last_w[b]
                self._need(waits, eng, ow, vv)
        for b in writes:
            if b in self.last_w:
                ow, vv = self.last_w[b]
                self._need(waits, eng, ow, vv)
            for ow, vv in self.readers.get(b, {}).items():
                self._need(waits, eng, ow, vv)
        if sem_as is not None:
            self.extra[sem_as] = self.extra.get(sem_as, 0) + CC_WAIT
            owner, val = sem_as, self.extra[sem_as]
        elif inc == 16:
            slot = self.dmacnt[eng] % NDQ
            self.dmacnt[eng] += 1
            key = (eng, slot)
            prev = self.dquses.get(key, 0)
            if prev:
                self._need(waits, eng, f"{eng}_dq{slot}", prev * 16)
            self.dquses[key] = prev + 1
            owner, val = f"{eng}_dq{slot}", (prev + 1) * 16
        else:
            self.count[eng] += inc
            owner, val = eng, self.count[eng]
        self.ops[eng].append((waits, fn, owner))
        for b in writes:
            self.last_w[b] = (owner, val)
            self.readers[b] = {}
        for b in reads:
            self.readers.setdefault(b, {})[owner] = val

    def emit(self, block, handles):
        def sem(owner):
            return handles[owner + "_sem"]
        for eng in ENGINES:
            def make(eng=eng, oplist=self.ops[eng]):
                def body(h):
                    for waits, fn, owner in oplist:
                        for ow in sorted(waits):
                            h.wait_ge(sem(ow), waits[ow])
                        inst = fn(h)
                        if inst is not None:
                            if owner in self.extra:
                                inst.then_inc(sem(owner))
                            else:
                                inst.then_inc(sem(owner), 16 if "_dq" in owner else 1)
                    if eng == "sync":
                        for oe in ENGINES:
                            if oe != "sync" and self.count[oe]:
                                h.wait_ge(sem(oe), self.count[oe])
                        for name, cnt in self.extra.items():
                            h.wait_ge(sem(name), cnt)
                        for (qe, slot), uses in self.dquses.items():
                            h.wait_ge(sem(f"{qe}_dq{slot}"), uses * 16)
                        if self.count["sync"]:
                            h.wait_ge(sem("sync"), self.count["sync"])
                return body
            getattr(block, eng)(make())


def build_nc(n_blocks=NB, do_ffn=True):
    nc = bass.Bass("TRN2", target_bir_lowering=False)
    dram = {}

    def din(name, shape, dt=F32):
        dram[name] = nc.dram_tensor(name, shape, dt, kind="ExternalInput").ap()
        return dram[name]

    def dint(name, shape, dt=F32):
        dram[name] = nc.dram_tensor(name, shape, dt, kind="Internal").ap()
        return dram[name]

    x_in = din("x", [T_CORE, C])
    for v in ("maa_x", "maa_w", "maa_k", "maa_v", "maa_r", "t_decay", "ln_w", "ln_b"):
        din(v, [C])
    din("maa_w1", [C, 128])
    din("maa_w2", [4, 32, C])
    din("dw1", [C, 64])
    din("dw2", [64, C])
    din("faaaa", [H, HD])
    for v in ("W_r", "W_k", "W_v", "W_o"):
        din(v, [C, C])
    din("gate_w", [E, C])
    din("wgu_own", [EPC, C, 2 * I_FF])
    din("wd_own", [EPC, I_FF, C])
    din("wsgu", [C, 2 * I_FF])
    din("wsd", [I_FF, C])
    din("meta_ind", [8, 1])
    din("meta_cols", [128, 8])
    y_out = nc.dram_tensor("y", [T_CORE, C], F32, kind="ExternalOutput").ap()

    xln_d = dint("xln_d", [T_CORE, C], BF16)
    yn_d = dint("yn_d", [T_CORE, C], BF16)
    kbar_d = dint("kbar_d", [C, TB], BF16)
    xn_d = dint("xn_d", [T_CORE, C], BF16)
    xn_all = dint("xn_all", [TOT + 128, C], BF16)
    cnt_d = dint("cnt_d", [1, E])
    cnt8_d = dint("cnt8_d", [NCORE, E])
    slotmap4_d = [dint(f"slotmap{kk}_d", [E * CAP + 128, 2]) for kk in range(4)]
    slotmapG4_d = [dint(f"slotmapG{kk}_d", [E * CAP, 2]) for kk in range(4)]
    cap_d = dint("cap_d", [EPC * CAP, C], BF16)
    eout_d = dint("eout_d", [EPC * CAP, C], BF16)
    eoutG_d = dint("eoutG_d", [E * CAP + 128, C], BF16)
    sh_d = dint("sh_d", [T_CORE, C], BF16)
    x2_d = dint("x2_d", [T_CORE, C])

    p = Prog()
    handles = {}
    es = ExitStack()

    def sb(name, shape, dt):
        t = es.enter_context(nc.sbuf_tensor(name, shape, dt))
        handles[name] = t
        return t

    def psum(name, shape, dt=F32):
        t = es.enter_context(nc.psum_tensor(name, shape, dt))
        handles[name] = t
        return t

    with es:
        for e in ENGINES:
            handles[e + "_sem"] = es.enter_context(nc.semaphore(e + "_sem"))
        for cc in ("ag0", "ag1", "ar1", "ag2"):
            handles[cc + "_sem"] = es.enter_context(nc.semaphore(cc + "_sem"))
        for e in ("sync", "scalar", "gpsimd"):
            for i in range(NDQ):
                handles[f"{e}_dq{i}_sem"] = es.enter_context(
                    nc.semaphore(f"{e}_dq{i}_sem"))

        vecs = sb("vecs", [128, 8 * KC], F32)
        VEC = {n: i for i, n in enumerate(
            ("maa_x", "maa_w", "maa_k", "maa_v", "maa_r", "t_decay", "ln_w", "ln_b"))}

        def vcol(vn, k):
            i = VEC[vn] * KC + k
            return vecs[:, i:i + 1]

        ident_f = sb("ident_f", [128, 128], F32)
        trilmask = sb("trilmask", [128, 128], BF16)
        stril_f = sb("stril_f", [128, 128], F32)
        ones1b = sb("ones1b", [1, 128], BF16)
        ones1f = sb("ones1f", [1, 128], F32)
        onescol_b = sb("onescol_b", [128, 1], BF16)
        onescol_f = sb("onescol_f", [128, 1], F32)
        ublk = sb("ublk", [128, H], BF16)
        iotak = sb("iotak", [128, E], F32)
        iotak1024 = sb("iotak1024", [128, E], F32)
        iota_tg1 = sb("iota_tg1", [128, NCH], F32)
        iotasch = sb("iotasch", [128, 8], F32)
        trash_iota = sb("trash_iota", [128, NCH * 4], F32)
        scanmask = sb("scanmask", [128, TB], F32)
        meta_sb = sb("meta_sb", [128, 8], F32)
        consts = sb("consts", [128, 4], F32)
        ind_sb = sb("ind_sb", [8, 1], F32)
        zero_sb = sb("zero_sb", [128, 512], F32)

        w_r = sb("w_r", [128, KC * C], BF16)
        w_k = sb("w_k", [128, KC * C], BF16)
        w_v = sb("w_v", [128, KC * C], BF16)
        w_o = sb("w_o", [128, KC * C], BF16)
        w_m1 = sb("w_m1", [128, KC * 128], BF16)
        w_m2 = sb("w_m2", [64, 2 * C], BF16)
        w_d1 = sb("w_d1", [128, KC * 64], BF16)
        w_d2 = sb("w_d2", [64, C], BF16)
        gwT = sb("gwT", [128, KC * E], BF16)
        btil = sb("btil", [1, C], BF16)
        lnbstage = sb("lnbstage", [128, KC], BF16)

        x_tok = sb("x_tok", [128, 2 * C], F32)
        bn6 = sb("bn6", [128, 48], F32)
        bn2 = sb("bn2", [128, 8], F32)
        mcol = sb("mcol", [128, 1], F32)
        rstdcol = sb("rstdcol", [128, 3], F32)
        tcol = sb("tcol", [128, 1], F32)
        xln_tok = sb("xln_tok", [128, 2 * C], BF16)
        W1 = 544
        HC = 15
        D0 = 16
        xlnfm = sb("xlnfm", [128, KC * W1], BF16)
        xxfm = sb("xxfm", [128, KC * TB], BF16)
        xxx_t = sb("xxx_t", [128, 2 * TB], BF16)
        af_t = sb("af_t", [128, 2 * TB], BF16)
        xf_t = sb("xf_t", [128, KC * TB], BF16)
        tanhA = sb("tanhA", [64, TB], BF16)
        tanhB = sb("tanhB", [64, TB], BF16)
        tanhw = tanhB
        p_t = sb("p_t", [128, TB], F32)
        incl_t = sb("incl_t", [128, TB], F32)
        rs_t = sb("rs_t", [128, TB], BF16)
        ks_t = sb("ks_t", [128, TB], BF16)
        dend = sb("dend", [128, KC * CPB], F32)
        r_fm = sb("r_fm", [128, KC * TB], BF16)
        k_fm = sb("k_fm", [128, KC * TB], BF16)
        ruk_fm = sb("ruk_fm", [128, KC * TB], BF16)
        kb_fm = sb("kb_fm", [128, 2 * TB], BF16)
        kb_tok = sb("kb_tok", [128, 2 * C], BF16)
        v_tok = sb("v_tok", [128, CPB * C], BF16)
        abuf = sb("abuf", [128, 4 * 512], BF16)
        d_sb = sb("d_sb", [128, H], BF16)
        dv_t = sb("dv_t", [128, C], BF16)
        y_sb = sb("y_sb", [128, C], F32)
        gn16a = sb("gn16a", [128, H], F32)
        gn16b = sb("gn16b", [128, H], F32)
        gnal = sb("gnal", [128, H], F32)
        gnbe = sb("gnbe", [128, H], F32)
        yn_tok = sb("yn_tok", [128, C], BF16)
        ynfm = sb("ynfm", [128, KC * TB], BF16)
        S_sb = sb("S_sb", [128, 8 * HD], F32)
        S_bf = sb("S_bf", [128, 8 * HD], BF16)
        Sdec = sb("Sdec", [128, 8 * HD], F32)
        ao_sb = sb("ao_sb", [128, C], F32)
        aoacc = sb("aoacc", [128, 4], F32)
        sqscr = sb("sqscr", [128, 512], F32)
        x2_sb = sb("x2_sb", [128, C], F32)
        xn_tok = xln_tok[:, 0:C]
        h_fm = xxfm
        NROUT = 4
        rout_t = [xf_t[:, i * C:(i + 1) * C] for i in range(NROUT)]
        fin_t = y_sb
        cnt8_sb = sb("cnt8_sb", [8, E], F32)
        cntrow = sb("cntrow", [1, E], F32)
        maskf = sb("maskf", [128, E], F32)
        smg_all = sb("smg_all", [128, EPC * 8], F32)
        capidx_all = sb("capidx_all", [128, EPC * 8], I32)
        smrows_all = sb("smrows_all", [128, EPC * 8 * 4 * 2], F32)
        capf2_all = sb("capf2_all", [128, EPC * 8 * 2], F32)
        capf = sb("capf", [128, EPC * 8], F32)
        ustage = sb("ustage", [16, HD], F32)

        vecstage = y_sb[0:8, 0:1024]
        gstage = x2_sb[0:32, 0:1024]
        chpfx = ao_sb[0:1, 0:NCH * E]
        rf32 = r_fm.ap().bitcast(F32)
        kf32 = k_fm.ap().bitcast(F32)
        rkf32 = ruk_fm.ap().bitcast(F32)
        vtf32 = v_tok.ap().bitcast(F32)
        logits_sb = rf32[:, 0:NCH * E]
        m8_sb = rf32[:, 1024:1024 + NCH * 8]
        thrT = rf32[:, 1280:1280 + NCH]
        m1T = rf32[:, 1312:1312 + NCH]
        posg_sb = kf32[:, 0:NCH * E]
        gates_sb = kf32[:, 1024:2048]
        tbl_sb = rkf32[:, 0:NCH * E]
        keep_sb = rkf32[:, 1024:2048]
        maskb = v_tok[:, 0:NCH * E]
        slotcols = vtf32[:, 512:512 + NCH * 4]
        gcols = vtf32[:, 640:640 + NCH * 4]
        sc_all = vtf32[:, 768:768 + NCH * 8]
        slotints = vtf32[:, 1024:1024 + NCH * 4].bitcast(I32)

        w_gu_e = w_r
        w_d_e = w_k
        capfm = w_v
        w_sgu_a = w_o
        w_sd_a = xlnfm

        pbb = [psum("pb0", [128, 512], F32), psum("pb1", [128, 512], F32)]
        pbA = [psum("pbA0", [128, 512], F32), psum("pbA1", [128, 512], F32)]
        pbY = [psum("pbY0", [128, 512], F32), psum("pbY1", [128, 512], F32)]
        pbS = psum("pbS", [128, 512], F32)
        pbsm = psum("pbsm", [128, 512], F32)

        block = es.enter_context(nc.Block())

        p.op("gpsimd", lambda g: g.memset(ident_f[:, :], 1.0), writes=("ident_f",))
        p.op("gpsimd", lambda g: g.affine_select(
            ident_f[:, :], ident_f[:, :], pattern=[[1, 128]], compare_op=OP.is_equal,
            fill=0.0, base=0, channel_multiplier=-1),
            reads=("ident_f",), writes=("ident_f",))
        p.op("gpsimd", lambda g: g.memset(trilmask[:, :], 1.0), writes=("trilmask",))
        p.op("gpsimd", lambda g: g.affine_select(
            trilmask[:, :], trilmask[:, :], pattern=[[1, 128]], compare_op=OP.is_gt,
            fill=0.0, base=0, channel_multiplier=-1),
            reads=("trilmask",), writes=("trilmask",))
        p.op("gpsimd", lambda g: g.memset(stril_f[:, :], 1.0), writes=("stril_f",))
        p.op("gpsimd", lambda g: g.affine_select(
            stril_f[:, :], stril_f[:, :], pattern=[[1, 128]], compare_op=OP.is_gt,
            fill=0.0, base=0, channel_multiplier=-1),
            reads=("stril_f",), writes=("stril_f",))
        for tile, key in ((ones1b, "ones1b"), (ones1f, "ones1f"),
                          (onescol_b, "onescol_b"), (onescol_f, "onescol_f")):
            p.op("gpsimd", lambda g, tile=tile: g.memset(tile[:, :], 1.0), writes=(key,))
        p.op("gpsimd", lambda g: g.memset(zero_sb[:, :], 0.0), writes=("zero_sb",))
        for ci, cval in enumerate((LN_EPS, GN_EPS, 0.0, 1.0)):
            p.op("gpsimd", lambda g, ci=ci, cval=cval: g.memset(
                consts[:, ci:ci + 1], cval), writes=("consts",))
            nc.const_aps.register(F32, cval, consts[:, ci:ci + 1])
        p.op("gpsimd", lambda g: g.iota(
            iotak[:, :], pattern=[[CAP, E]], base=0, channel_multiplier=0,
            allow_small_or_imprecise_dtypes=True), writes=("iotak",))
        p.op("gpsimd", lambda g: g.iota(
            iotak1024[:, :], pattern=[[CAP, E]], base=CAP, channel_multiplier=0,
            allow_small_or_imprecise_dtypes=True), writes=("iotak1024",))
        p.op("gpsimd", lambda g: g.iota(
            iota_tg1[:, :], pattern=[[128, NCH]], base=1, channel_multiplier=1,
            allow_small_or_imprecise_dtypes=True), writes=("iota_tg1",))
        p.op("gpsimd", lambda g: g.iota(
            iotasch[:, :], pattern=[[128, 8]], base=0, channel_multiplier=1,
            allow_small_or_imprecise_dtypes=True), writes=("iotasch",))
        p.op("gpsimd", lambda g: g.iota(
            trash_iota[:, :], pattern=[[1, NCH * 4]], base=E * CAP,
            channel_multiplier=0, allow_small_or_imprecise_dtypes=True),
            writes=("trash_iota",))
        p.op("gpsimd", lambda g: g.memset(scanmask[:, :], 1.0), writes=("scanmask",))
        p.op("gpsimd", lambda g: g.memset(
            scanmask.ap().rearrange("q (a b) -> q a b", b=L)[:, :, 0:1], 0.0),
            reads=("scanmask",), writes=("scanmask",))

        p.op("sync", lambda s: s.dma_start(meta_sb[:, :], dram["meta_cols"]),
             writes=("meta_sb",), inc=16)
        p.op("sync", lambda s: s.dma_start(ind_sb[:, :], dram["meta_ind"]),
             writes=("ind_sb",), inc=16)
        p.op("vector", lambda v: v.tensor_scalar(
            iota_tg1[:, :], iota_tg1[:, :], meta_sb[:, 0:1], None, OP.add),
            reads=("iota_tg1", "meta_sb"), writes=("iota_tg1",))

        for vi, vn in enumerate(VEC):
            src = dram[vn].rearrange("(a b) -> a b", b=128)
            p.op("sync", lambda s, src=src, vi=vi: s.dma_start(
                vecstage[:, vi * 128:(vi + 1) * 128], src),
                writes=("y_sb",), inc=16)
        for vi in range(8):
            p.op("tensor", lambda t, vi=vi: t.transpose(
                pbsm[:, vi * 8: vi * 8 + 8], vecstage[0:8, vi * 128:(vi + 1) * 128],
                ident_f[0:8, 0:8]),
                reads=("y_sb", "ident_f"), writes=("pbsm",))
        p.op("vector", lambda v: v.tensor_copy(vecs[:, 0:64], pbsm[:, 0:64]),
             reads=("pbsm",), writes=("vecs",))

        p.op("sync", lambda s: s.dma_start(ustage[:, :], dram["faaaa"]),
             writes=("ustage",), inc=16)
        p.op("tensor", lambda t: t.transpose(
            pbsm[0:64, 64:80], ustage[0:16, :], ident_f[0:16, 0:16]),
            reads=("ustage", "ident_f"), writes=("pbsm",))
        p.op("gpsimd", lambda g: g.memset(ublk[:, :], 0.0), writes=("ublk",))
        for g_ in range(8):
            p.op("vector", lambda v, g_=g_: v.tensor_copy(
                ublk[0:64, 2 * g_:2 * g_ + 1], pbsm[0:64, 64 + 2 * g_:65 + 2 * g_]),
                reads=("pbsm",), writes=("ublk",))
            p.op("vector", lambda v, g_=g_: v.tensor_copy(
                ublk[64:128, 2 * g_ + 1:2 * g_ + 2], pbsm[0:64, 65 + 2 * g_:66 + 2 * g_]),
                reads=("pbsm",), writes=("ublk",))

        def wload(dst, src_name, cols, key):
            src = dram[src_name].rearrange("(a b) c -> b a c", b=128)
            p.op("gpsimd", lambda g: g.dma_start(
                dst.ap().rearrange("q (a c) -> q a c", c=cols), src),
                writes=(key,), inc=16)

        wload(w_r, "W_r", C, "w_r")
        wload(w_k, "W_k", C, "w_k")
        wload(w_v, "W_v", C, "w_v")
        wload(w_o, "W_o", C, "w_o")
        wload(w_m1, "maa_w1", 128, "w_m1")
        wload(w_d1, "dw1", 64, "w_d1")
        p.op("gpsimd", lambda g: g.dma_start(w_d2[:, :], dram["dw2"]),
             writes=("w_d2",), inc=16)
        p.op("gpsimd", lambda g: g.dma_start(
            w_m2[0:32, :].rearrange("q (a c) -> q a c", a=2),
            dram["maa_w2"].rearrange("(a two) q c -> q a c", two=2)[:, :, 0:C]
            if False else
            dram["maa_w2"].rearrange("(a two) q c -> two q a c", two=2)[0]),
            writes=("w_m2",), inc=16)
        p.op("gpsimd", lambda g: g.dma_start(
            w_m2[32:64, :].rearrange("q (a c) -> q a c", a=2),
            dram["maa_w2"].rearrange("(a two) q c -> two q a c", two=2)[1]),
            writes=("w_m2",), inc=16)

        p.op("sync", lambda s: s.dma_start(gstage[:, :], dram["gate_w"]),
             writes=("x2_sb",), inc=16)
        for k in range(KC):
            p.op("tensor", lambda t, k=k: t.transpose(
                pbsm[:, 128:160], gstage[0:32, k * 128:(k + 1) * 128],
                ident_f[0:32, 0:32]),
                reads=("x2_sb", "ident_f"), writes=("pbsm",))
            p.op("vector", lambda v, k=k: v.tensor_copy(
                gwT[:, k * E:(k + 1) * E], pbsm[:, 128:160]),
                reads=("pbsm",), writes=("gwT",))

        p.op("vector", lambda v: v.tensor_copy(
            lnbstage[:, :], vecs[:, VEC["ln_b"] * KC:(VEC["ln_b"] + 1) * KC]),
            reads=("vecs",), writes=("lnbstage",))
        for half in range(2):
            def fnB(t, half=half):
                inst = None
                for k in range(KC):
                    inst = t.matmul(
                        pbsm[0:1, 0:512], lnbstage[:, k:k + 1],
                        w_o[:, k * C + half * 512: k * C + (half + 1) * 512],
                        start=(k == 0), stop=(k == KC - 1))
                return inst
            p.op("tensor", fnB, reads=("lnbstage", "w_o"), writes=("pbsm",))
            p.op("vector", lambda v, half=half: v.tensor_copy(
                btil[0:1, half * 512:(half + 1) * 512], pbsm[0:1, 0:512]),
                reads=("pbsm",), writes=("btil",))
        for k in range(KC):
            p.op("vector", lambda v, k=k: v.tensor_scalar(
                w_o[:, k * C:(k + 1) * C], w_o[:, k * C:(k + 1) * C],
                vcol("ln_w", k), None, OP.mult),
                reads=("w_o", "vecs", "btil"), writes=("w_o",))

        for kk in range(4):
            p.op("sync", lambda s, kk=kk: s.dma_start(
                slotmap4_d[kk][0:E * CAP, :]
                .rearrange("(a q c2) c -> a q (c2 c)", q=128, c2=256)[0],
                zero_sb[:, :]), reads=("zero_sb",), writes=(f"smz{kk}",), inc=16)
            p.op("sync", lambda s, kk=kk: s.dma_start(
                slotmap4_d[kk][E * CAP:E * CAP + 128, :], zero_sb[:, 0:2]),
                reads=("zero_sb",), writes=(f"smz{kk}",), inc=16)
        for half in range(2):
            p.op("gpsimd", lambda g, half=half: g.dma_start(
                xn_all[TOT:TOT + 128, half * 512:(half + 1) * 512], zero_sb[:, :]),
                reads=("zero_sb",), writes=("xnall_z",), inc=16)
            p.op("gpsimd", lambda g, half=half: g.dma_start(
                eoutG_d[E * CAP:E * CAP + 128, half * 512:(half + 1) * 512],
                zero_sb[:, :]),
                reads=("zero_sb",), writes=("eoutG_z",), inc=16)

        for B in range(n_blocks):
            tok0 = B * TB
            seq_start = (B % (SEQ // TB)) == 0

            for j in range(CPB):
                ch = B * CPB + j
                sl = ch % 2
                xt = x_tok[:, sl * C:(sl + 1) * C]
                xlt = xln_tok[:, sl * C:(sl + 1) * C]
                b6 = bn6[:, sl * 12:sl * 12 + 12]
                b2 = bn2[:, sl * 2:sl * 2 + 2]
                rsd = rstdcol[:, sl:sl + 1]
                rows = x_in.rearrange("(a q) c -> a q c", q=128)[ch]
                p.op("sync", lambda s, rows=rows, xt=xt: s.dma_start(xt, rows),
                     writes=(f"x_tok{sl}",), inc=16)
                p.op("vector", lambda v, xt=xt, b6=b6: v.bn_stats(b6[:, 0:6], xt[:, 0:512]),
                     reads=(f"x_tok{sl}",), writes=(f"bn6{sl}",))
                p.op("vector", lambda v, xt=xt, b6=b6: v.bn_stats(b6[:, 6:12], xt[:, 512:1024]),
                     reads=(f"x_tok{sl}",), writes=(f"bn6{sl}",))
                p.op("vector", lambda v, b6=b6, b2=b2: v.bn_aggr(b2, b6),
                     reads=(f"bn6{sl}",), writes=(f"bn2{sl}",))
                p.op("scalar", lambda s, b2=b2, rsd=rsd: s.activation(
                    rsd, b2[:, 1:2], AF.Sqrt, bias=LN_EPS, scale=1.0),
                    reads=(f"bn2{sl}", "consts"), writes=(f"rstd{sl}",))
                p.op("vector", lambda v, rsd=rsd: v.reciprocal(rsd, rsd),
                     reads=(f"rstd{sl}",), writes=(f"rstd{sl}",))
                p.op("vector", lambda v, xt=xt, xlt=xlt, b2=b2, rsd=rsd: v.tensor_scalar(
                    xlt, xt, b2[:, 0:1], rsd, OP.subtract, OP.mult),
                    reads=(f"x_tok{sl}", f"bn2{sl}", f"rstd{sl}"), writes=(f"xlntok{sl}",))
                rows_o = xln_d.rearrange("(a q) c -> a q c", q=128)[ch]
                p.op("sync", lambda s, rows_o=rows_o, xlt=xlt: s.dma_start(rows_o, xlt),
                     reads=(f"xlntok{sl}",), writes=("xln_dram",), inc=16)

            for k in range(KC):
                if seq_start:
                    p.op("gpsimd", lambda g, k=k: g.memset(
                        xlnfm[:, k * W1 + HC:k * W1 + HC + 1], 0.0),
                        reads=("xlnfm",), writes=("xlnfm",))
                else:
                    p.op("vector", lambda v, k=k: v.tensor_copy(
                        xlnfm[:, k * W1 + HC:k * W1 + HC + 1],
                        xlnfm[:, k * W1 + D0 + TB - 1:k * W1 + D0 + TB]),
                        reads=("xlnfm",), writes=("xlnfm",))
            for k in range(KC):
                src = xln_d[tok0:tok0 + TB, k * 128:(k + 1) * 128]
                p.op("sync", lambda s, src=src, k=k: s.dma_start(
                    xlnfm[:, k * W1 + D0:k * W1 + D0 + TB], src, transpose=True),
                    reads=("xln_dram",), writes=("xlnfm",), inc=16)

            for k in range(KC):
                p.op("vector", lambda v, k=k: v.tensor_sub(
                    xxfm[:, k * TB:(k + 1) * TB],
                    xlnfm[:, k * W1 + HC:k * W1 + HC + TB],
                    xlnfm[:, k * W1 + D0:k * W1 + D0 + TB]),
                    reads=("xlnfm",), writes=("xxfm",))

            for k in range(KC):
                pk = k % 2
                p.op("vector", lambda v, k=k, pk=pk: v.scalar_tensor_tensor(
                    xxx_t[:, pk * TB:(pk + 1) * TB],
                    xxfm[:, k * TB:(k + 1) * TB], vcol("maa_x", k),
                    xlnfm[:, k * W1 + D0:k * W1 + D0 + TB],
                    OP.mult, OP.add),
                    reads=("xxfm", "vecs", "xlnfm"), writes=(f"xxx{pk}",))
                p.op("tensor", lambda t, k=k, pk=pk: t.matmul(
                    pbb[0][:, :], w_m1[:, k * 128:(k + 1) * 128],
                    xxx_t[:, pk * TB:(pk + 1) * TB],
                    start=(k == 0), stop=(k == KC - 1)),
                    reads=(f"xxx{pk}", "w_m1"), writes=("pb0",))
            p.op("scalar", lambda s: s.activation(
                tanhA[:, :], pbb[0][0:64, :], AF.Tanh), reads=("pb0",), writes=("tanhA",))
            p.op("scalar", lambda s: s.activation(
                tanhB[:, :], pbb[0][64:128, :], AF.Tanh), reads=("pb0",), writes=("tanhB",))

            tanh_f = [(tanhA, 0, "tanhA"), (tanhA, 32, "tanhA"),
                      (tanhB, 0, "tanhB"), (tanhB, 32, "tanhB")]

            for fi, fname in ((1, "k"), (2, "v"), (3, "r"), (0, "w")):
                ttile, trow, tkey = tanh_f[fi]
                vkey = {"w": "maa_w", "k": "maa_k", "v": "maa_v", "r": "maa_r"}[fname]
                for k in range(KC):
                    pk = k % 2
                    p.op("tensor", lambda t, k=k, ttile=ttile, trow=trow, fi=fi: t.matmul(
                        pbb[1][:, :],
                        w_m2[32 * (fi % 2):32 * (fi % 2) + 32,
                             (fi // 2) * C + k * 128: (fi // 2) * C + (k + 1) * 128],
                        ttile[trow:trow + 32, :], start=True, stop=True),
                        reads=(tkey, "w_m2"), writes=("pb1",))
                    p.op("scalar", lambda s, k=k, pk=pk, vkey=vkey: s.activation(
                        af_t[:, pk * TB:(pk + 1) * TB], pbb[1][:, :], AF.Identity,
                        bias=vcol(vkey, k), scale=1.0),
                        reads=("pb1", "vecs"), writes=(f"af{pk}",))
                    p.op("vector", lambda v, k=k, pk=pk: v.tensor_mul(
                        xf_t[:, k * TB:(k + 1) * TB], xxfm[:, k * TB:(k + 1) * TB],
                        af_t[:, pk * TB:(pk + 1) * TB]),
                        reads=(f"af{pk}", "xxfm"), writes=("xf",))
                    p.op("gpsimd", lambda g, k=k: g.tensor_add(
                        xf_t[:, k * TB:(k + 1) * TB], xf_t[:, k * TB:(k + 1) * TB],
                        xlnfm[:, k * W1 + D0:k * W1 + D0 + TB]),
                        reads=("xf", "xlnfm"), writes=("xf",))

                if fname == "w":
                    def fn_dw1(t):
                        inst = None
                        for k in range(KC):
                            inst = t.matmul(pbb[0][0:64, :],
                                            w_d1[:, k * 64:(k + 1) * 64],
                                            xf_t[:, k * TB:(k + 1) * TB],
                                            start=(k == 0), stop=(k == KC - 1))
                        return inst
                    p.op("tensor", fn_dw1, reads=("xf", "w_d1"), writes=("pb0",))
                    p.op("scalar", lambda s: s.activation(
                        tanhw[:, :], pbb[0][0:64, :], AF.Tanh),
                        reads=("pb0",), writes=("tanhB",))
                    for k in range(KC):
                        pk = k % 2
                        p.op("tensor", lambda t, k=k: t.matmul(
                            pbb[0][:, :], w_d2[:, k * 128:(k + 1) * 128], tanhw[:, :],
                            start=True, stop=True),
                            reads=("tanhB", "w_d2"), writes=("pb0",))
                        p.op("scalar", lambda s, k=k, pk=pk: s.activation(
                            p_t[:, :], pbb[0][:, :], AF.Exp,
                            bias=vcol("t_decay", k), scale=1.0),
                            reads=("pb0", "vecs"), writes=("p_t",))
                        p.op("vector", lambda v, pk=pk: v.tensor_tensor_scan(
                            incl_t[:, :], scanmask[:, :],
                            p_t[:, :], 0.0, OP.mult, OP.add),
                            reads=("p_t", "scanmask"), writes=("incl_t",))
                        p.op("scalar", lambda s, pk=pk: s.activation(
                            ks_t[:, :], incl_t[:, :], AF.Exp),
                            reads=("incl_t",), writes=("ks_t",))
                        p.op("scalar", lambda s, k=k, pk=pk: s.activation(
                            dend[:, k * CPB:(k + 1) * CPB],
                            incl_t[:, :]
                            .rearrange("q (a b) -> q a b", b=L)[:, :, L - 1:L].squeeze(2),
                            AF.Exp, bias=0.0, scale=-1.0),
                            reads=("incl_t", "consts"), writes=("dend",))
                        p.op("vector", lambda v, pk=pk: v.tensor_sub(
                            p_t[:, :], p_t[:, :],
                            incl_t[:, :]),
                            reads=("p_t", "incl_t"), writes=("p_t",))
                        p.op("scalar", lambda s, pk=pk: s.activation(
                            rs_t[:, :], p_t[:, :], AF.Exp),
                            reads=("p_t",), writes=("rs_t",))
                        p.op("vector", lambda v, k=k: v.tensor_mul(
                            ruk_fm[:, k * TB:(k + 1) * TB], r_fm[:, k * TB:(k + 1) * TB],
                            k_fm[:, k * TB:(k + 1) * TB]),
                            reads=("r_fm", "k_fm"), writes=("ruk_fm",))
                        p.op("gpsimd", lambda g, k=k, pk=pk: g.tensor_mul(
                            kb_fm[:, pk * TB:(pk + 1) * TB],
                            k_fm[:, k * TB:(k + 1) * TB], ks_t[:, :]),
                            reads=("k_fm", "ks_t"), writes=("kb_fm",))
                        p.op("gpsimd", lambda g, k=k, pk=pk: g.tensor_mul(
                            kb_fm[:, pk * TB:(pk + 1) * TB]
                            .rearrange("q (a b) -> q a b", b=L),
                            kb_fm[:, pk * TB:(pk + 1) * TB]
                            .rearrange("q (a b) -> q a b", b=L),
                            dend[:, k * CPB:(k + 1) * CPB].unsqueeze(2)
                            .to_broadcast([128, CPB, L])),
                            reads=("kb_fm", "dend"), writes=("kb_fm",))
                        p.op("sync", lambda s, k=k, pk=pk: s.dma_start(
                            kbar_d[k * 128:(k + 1) * 128, :],
                            kb_fm[:, pk * TB:(pk + 1) * TB]),
                            reads=("kb_fm",), writes=("kbar_dram",), inc=16)
                        p.op("vector", lambda v, k=k: v.tensor_mul(
                            k_fm[:, k * TB:(k + 1) * TB], k_fm[:, k * TB:(k + 1) * TB],
                            ks_t[:, :]),
                            reads=("k_fm", "ks_t"), writes=("k_fm",))
                        p.op("vector", lambda v, k=k: v.tensor_mul(
                            r_fm[:, k * TB:(k + 1) * TB], r_fm[:, k * TB:(k + 1) * TB],
                            rs_t[:, :]),
                            reads=("r_fm", "rs_t"), writes=("r_fm",))
                elif fname in ("k", "r"):
                    wmat = w_k if fname == "k" else w_r
                    wkey = "w_k" if fname == "k" else "w_r"
                    dst = k_fm if fname == "k" else r_fm
                    dkey = "k_fm" if fname == "k" else "r_fm"
                    for o in range(KC):
                        po = o % 2
                        pt = pbb[po]

                        def fn(t, o=o, pt=pt, wmat=wmat):
                            inst = None
                            for k in range(KC):
                                inst = t.matmul(
                                    pt[:, :],
                                    wmat[:, k * C + o * 128: k * C + o * 128 + 128],
                                    xf_t[:, k * TB:(k + 1) * TB],
                                    start=(k == 0), stop=(k == KC - 1))
                            return inst
                        p.op("tensor", fn, reads=("xf", wkey), writes=(f"pb{po}",))
                        p.op("scalar", lambda s, o=o, pt=pt, dst=dst: s.activation(
                            dst[:, o * TB:(o + 1) * TB], pt[:, :], AF.Copy),
                            reads=(f"pb{po}",), writes=(dkey,))
                else:
                    for j in range(CPB):
                        for half in range(2):
                            pt = pbb[half]

                            def fn(t, j=j, half=half, pt=pt):
                                inst = None
                                for k in range(KC):
                                    inst = t.matmul(
                                        pt[:, :],
                                        xf_t[:, k * TB + j * 128: k * TB + (j + 1) * 128],
                                        w_v[:, k * C + half * 512: k * C + (half + 1) * 512],
                                        start=(k == 0), stop=(k == KC - 1))
                                return inst
                            p.op("tensor", fn, reads=("xf", "w_v"), writes=(f"pb{half}",))
                            p.op("scalar", lambda s, j=j, half=half, pt=pt: s.activation(
                                v_tok[:, j * C + half * 512: j * C + (half + 1) * 512],
                                pt[:, :], AF.Copy),
                                reads=(f"pb{half}",), writes=("v_tok",))

            if seq_start:
                p.op("gpsimd", lambda g: g.memset(S_sb[:, :], 0.0),
                     reads=("S_sb",), writes=("S_sb",))
                p.op("gpsimd", lambda g: g.memset(S_bf[:, :], 0.0),
                     reads=("S_bf",), writes=("S_bf",))
            for j in range(CPB):
                ch = B * CPB + j
                jp = j % 2
                p.op("sync", lambda s, j=j, jp=jp: s.dma_start(
                    kb_tok[:, jp * C:(jp + 1) * C], kbar_d[:, j * 128:(j + 1) * 128],
                    transpose=True),
                    reads=("kbar_dram",), writes=(f"kbt{jp}",), inc=16)
                def apos(h_):
                    return (h_ // 8) * 1024 + (h_ % 2) * 512 + ((h_ % 8) // 2) * 128
                for w_ in range(2):
                    def fnA(t, w_=w_, j=j):
                        inst = None
                        for par in range(2):
                            for idx in range(4):
                                h_ = 8 * w_ + 2 * idx + par
                                o = h_ // 2
                                rb = 64 * par
                                inst = t.matmul(
                                    pbA[par][:, idx * 128:(idx + 1) * 128],
                                    k_fm[rb:rb + 64, o * TB + j * L: o * TB + (j + 1) * L],
                                    r_fm[rb:rb + 64, o * TB + j * L: o * TB + (j + 1) * L],
                                    start=True, stop=True,
                                    tile_position=(rb, 0))
                        return inst
                    p.op("tensor", fnA, reads=("k_fm", "r_fm"),
                         writes=("pbA0", "pbA1"))
                    for par in range(2):
                        p.op("vector", lambda v, w_=w_, par=par: v.tensor_mul(
                            abuf[:, w_ * 1024 + par * 512: w_ * 1024 + (par + 1) * 512]
                            .rearrange("q2 (a b) -> q2 a b", b=128),
                            pbA[par].ap().rearrange("q2 (a b) -> q2 a b", b=128),
                            trilmask.ap().unsqueeze(1).to_broadcast([128, 4, 128])),
                            reads=(f"pbA{par}", "trilmask"), writes=("abuf",))

                def fnD(t, j=j):
                    inst = None
                    for g_ in range(8):
                        inst = t.matmul(
                            pbsm[:, 2 * g_:2 * g_ + 2],
                            ruk_fm[:, g_ * TB + j * L: g_ * TB + (j + 1) * L],
                            ublk[:, 2 * g_:2 * g_ + 2],
                            start=True, stop=True)
                    return inst
                p.op("tensor", fnD, reads=("ruk_fm", "ublk"), writes=("pbsm",))
                p.op("vector", lambda v: v.tensor_copy(d_sb[:, :], pbsm[:, 0:H]),
                     reads=("pbsm",), writes=("d_sb",))

                for par in range(2):
                    pY = pbY[par]

                    def fnY(t, par=par, j=j, pY=pY):
                        inst = None
                        rb = 64 * par
                        for i8 in range(8):
                            h_ = 2 * i8 + par
                            o = i8
                            t.matmul(
                                pY[:, i8 * 64:(i8 + 1) * 64],
                                r_fm[rb:rb + 64, o * TB + j * L: o * TB + (j + 1) * L],
                                S_bf[rb:rb + 64, o * 64:(o + 1) * 64],
                                start=True, stop=False,
                                tile_position=(rb, 0))
                            inst = t.matmul(
                                pY[:, i8 * 64:(i8 + 1) * 64],
                                abuf[:, apos(h_): apos(h_) + 128],
                                v_tok[:, j * C + h_ * 64: j * C + (h_ + 1) * 64],
                                start=False, stop=True)
                        return inst
                    p.op("tensor", fnY, reads=("abuf", "r_fm", "v_tok", "S_bf"),
                         writes=(f"pbY{par}",))

                def fnS(t, j=j, jp=jp):
                    inst = None
                    for g_ in range(8):
                        for sub in range(2):
                            h_ = 2 * g_ + sub
                            inst = t.matmul(
                                pbS[sub * 64:(sub + 1) * 64, g_ * 64:(g_ + 1) * 64],
                                kb_tok[:, jp * C + h_ * 64:jp * C + (h_ + 1) * 64],
                                v_tok[:, j * C + h_ * 64:j * C + (h_ + 1) * 64],
                                start=True, stop=True,
                                tile_position=(0, sub * 64))
                    return inst
                p.op("tensor", fnS, reads=(f"kbt{jp}", "v_tok"), writes=("pbS",))
                p.op("gpsimd", lambda g, j=j: g.tensor_mul(
                    Sdec.ap().rearrange("q (a b) -> q a b", b=HD),
                    S_sb.ap().rearrange("q (a b) -> q a b", b=HD),
                    dend.ap().rearrange("q (a b) -> q b a", b=CPB)[:, j, :]
                    .unsqueeze(2).to_broadcast([128, KC, HD])),
                    reads=("S_sb", "dend"), writes=("Sdec",))
                p.op("vector", lambda v: v.tensor_add(S_sb[:, :], pbS[:, :], Sdec[:, :]),
                     reads=("pbS", "Sdec"), writes=("S_sb",))
                p.op("vector", lambda v: v.tensor_copy(S_bf[:, :], S_sb[:, :]),
                     reads=("S_sb",), writes=("S_bf",))

                p.op("vector", lambda v, j=j: v.tensor_mul(
                    dv_t.ap().rearrange("q (a b) -> q a b", b=HD),
                    v_tok[:, j * C:(j + 1) * C].rearrange("q (a b) -> q a b", b=HD),
                    d_sb.ap().unsqueeze(2).to_broadcast([128, H, HD])),
                    reads=("v_tok", "d_sb"), writes=("dv_t",))
                for par in range(2):
                    p.op("vector", lambda v, par=par: v.tensor_add(
                        y_sb.ap().rearrange("q (a p d) -> q a p d", p=2, d=HD)[:, :, par, :],
                        pbY[par].ap().rearrange("q (a d) -> q a d", d=HD),
                        dv_t.ap().rearrange("q (a p d) -> q a p d", p=2, d=HD)[:, :, par, :]),
                        reads=(f"pbY{par}", "dv_t"), writes=("y_sb",))

                p.op("vector", lambda v: v.tensor_reduce(
                    gn16a[:, :], y_sb.ap().rearrange("q (h d) -> q h d", h=H),
                    AX.X, OP.add), reads=("y_sb",), writes=("gn16a",))
                p.op("scalar", lambda s: s.activation(
                    sqscr[:, :], y_sb[:, 0:512], AF.Square),
                    reads=("y_sb",), writes=("sqscr",))
                p.op("vector", lambda v: v.tensor_reduce(
                    gn16b[:, 0:8], sqscr.ap().rearrange("q (h d) -> q h d", h=8),
                    AX.X, OP.add), reads=("sqscr",), writes=("gn16b",))
                p.op("scalar", lambda s: s.activation(
                    sqscr[:, :], y_sb[:, 512:1024], AF.Square),
                    reads=("y_sb", "gn16b"), writes=("sqscr",))
                p.op("vector", lambda v: v.tensor_reduce(
                    gn16b[:, 8:16], sqscr.ap().rearrange("q (h d) -> q h d", h=8),
                    AX.X, OP.add), reads=("sqscr",), writes=("gn16b",))
                p.op("vector", lambda v: v.tensor_scalar(
                    gn16a[:, :], gn16a[:, :], 1.0 / HD, None, OP.mult),
                    reads=("gn16a",), writes=("gn16a",))
                p.op("vector", lambda v: v.tensor_mul(
                    gnal[:, :], gn16a[:, :], gn16a[:, :]),
                    reads=("gn16a",), writes=("gnal",))
                p.op("vector", lambda v: v.scalar_tensor_tensor(
                    gnal[:, :], gn16b[:, :], 1.0 / HD, gnal[:, :],
                    OP.mult, OP.subtract),
                    reads=("gn16b", "gnal"), writes=("gnal",))
                p.op("scalar", lambda s: s.activation(
                    gnal[:, :], gnal[:, :], AF.Sqrt, bias=GN_EPS, scale=1.0),
                    reads=("gnal", "consts"), writes=("gnal",))
                p.op("vector", lambda v: v.reciprocal(gnal[:, :], gnal[:, :]),
                     reads=("gnal",), writes=("gnal",))
                p.op("vector", lambda v: v.tensor_mul(
                    gnbe[:, :], gn16a[:, :], gnal[:, :]),
                    reads=("gn16a", "gnal"), writes=("gnbe",))
                p.op("vector", lambda v: v.tensor_mul(
                    yn_tok.ap().rearrange("q (h d) -> q h d", h=H),
                    y_sb.ap().rearrange("q (h d) -> q h d", h=H),
                    gnal.ap().unsqueeze(2).to_broadcast([128, H, HD])),
                    reads=("y_sb", "gnal"), writes=("yn_tok",))
                p.op("vector", lambda v: v.tensor_sub(
                    yn_tok.ap().rearrange("q (h d) -> q h d", h=H),
                    yn_tok.ap().rearrange("q (h d) -> q h d", h=H),
                    gnbe.ap().unsqueeze(2).to_broadcast([128, H, HD])),
                    reads=("yn_tok", "gnbe"), writes=("yn_tok",))
                rows_o = yn_d.rearrange("(a q) c -> a q c", q=128)[ch]
                p.op("sync", lambda s, rows_o=rows_o: s.dma_start(rows_o, yn_tok[:, :]),
                     reads=("yn_tok",), writes=("yn_dram",), inc=16)

            for k in range(KC):
                src = yn_d[tok0:tok0 + TB, k * 128:(k + 1) * 128]
                p.op("sync", lambda s, src=src, k=k: s.dma_start(
                    ynfm[:, k * TB:(k + 1) * TB], src, transpose=True),
                    reads=("yn_dram", "ynfm"), writes=("ynfm",), inc=16)

            for j in range(CPB):
                ch = B * CPB + j
                for half in range(2):
                    pt = pbb[half]

                    def fnO(t, j=j, half=half, pt=pt):
                        for k in range(KC):
                            t.matmul(
                                pt[:, :],
                                ynfm[:, k * TB + j * 128: k * TB + (j + 1) * 128],
                                w_o[:, k * C + half * 512: k * C + (half + 1) * 512],
                                start=(k == 0), stop=False)
                        return t.matmul(pt[:, :], ones1b[0:1, :],
                                        btil[0:1, half * 512:(half + 1) * 512],
                                        start=False, stop=True)
                    p.op("tensor", fnO, reads=("ynfm", "w_o", "btil", "ones1b"),
                         writes=(f"pb{half}",))
                    p.op("scalar", lambda s, half=half, pt=pt: s.activation(
                        ao_sb[:, half * 512:(half + 1) * 512], pt[:, :], AF.Copy,
                        accum_out=aoacc[:, half:half + 1]),
                        reads=(f"pb{half}",), writes=("ao_sb", "aoacc"))
                for half in range(2):
                    p.op("scalar", lambda s, half=half: s.activation(
                        sqscr[:, :], ao_sb[:, half * 512:(half + 1) * 512], AF.Square,
                        accum_out=aoacc[:, 2 + half:3 + half]),
                        reads=("ao_sb",), writes=("sqscr", "aoacc"))
                p.op("vector", lambda v: v.tensor_add(
                    mcol[:, :], aoacc[:, 0:1], aoacc[:, 1:2]),
                    reads=("aoacc",), writes=("mcol",))
                p.op("vector", lambda v: v.tensor_scalar(
                    mcol[:, :], mcol[:, :], 1.0 / C, None, OP.mult),
                    reads=("mcol",), writes=("mcol",))
                p.op("vector", lambda v: v.tensor_add(
                    rstdcol[:, 2:3], aoacc[:, 2:3], aoacc[:, 3:4]),
                    reads=("aoacc",), writes=("rstdO",))
                p.op("vector", lambda v: v.tensor_mul(
                    tcol[:, :], mcol[:, :], mcol[:, :]),
                    reads=("mcol",), writes=("tcol",))
                p.op("vector", lambda v: v.scalar_tensor_tensor(
                    rstdcol[:, 2:3], rstdcol[:, 2:3], 1.0 / C, tcol[:, :],
                    OP.mult, OP.subtract),
                    reads=("rstdO", "tcol"), writes=("rstdO",))
                p.op("scalar", lambda s: s.activation(
                    rstdcol[:, 2:3], rstdcol[:, 2:3], AF.Sqrt, bias=LN_EPS, scale=1.0),
                    reads=("rstdO", "consts"), writes=("rstdO",))
                p.op("vector", lambda v: v.reciprocal(rstdcol[:, 2:3], rstdcol[:, 2:3]),
                     reads=("rstdO",), writes=("rstdO",))
                rows = x_in.rearrange("(a q) c -> a q c", q=128)[ch]
                p.op("scalar", lambda s, rows=rows: s.dma_start(x2_sb[:, :], rows),
                     writes=("x2_sb",), inc=16)
                p.op("vector", lambda v: v.tensor_scalar(
                    ao_sb[:, :], ao_sb[:, :], mcol[:, :], rstdcol[:, 2:3],
                    OP.subtract, OP.mult),
                    reads=("ao_sb", "mcol", "rstdO"), writes=("ao_sb",))
                p.op("vector", lambda v: v.tensor_add(
                    x2_sb[:, :], x2_sb[:, :], ao_sb[:, :]),
                    reads=("x2_sb", "ao_sb"), writes=("x2_sb",))
                rows_o = x2_d.rearrange("(a q) c -> a q c", q=128)[ch]
                p.op("sync", lambda s, rows_o=rows_o: s.dma_start(rows_o, x2_sb[:, :]),
                     reads=("x2_sb",), writes=("x2_dram",), inc=16)
                p.op("vector", lambda v: v.bn_stats(bn6[:, 24:30], x2_sb[:, 0:512]),
                     reads=("x2_sb",), writes=("bn6L2",))
                p.op("vector", lambda v: v.bn_stats(bn6[:, 30:36], x2_sb[:, 512:1024]),
                     reads=("x2_sb",), writes=("bn6L2",))
                p.op("vector", lambda v: v.bn_aggr(bn2[:, 4:6], bn6[:, 24:36]),
                     reads=("bn6L2",), writes=("bn2L2",))
                p.op("scalar", lambda s: s.activation(
                    tcol[:, :], bn2[:, 5:6], AF.Sqrt, bias=LN_EPS, scale=1.0),
                    reads=("bn2L2", "consts"), writes=("tcol",))
                p.op("vector", lambda v: v.reciprocal(tcol[:, :], tcol[:, :]),
                     reads=("tcol",), writes=("tcol",))
                p.op("vector", lambda v: v.tensor_scalar(
                    xn_tok, x2_sb[:, :], bn2[:, 4:5], tcol[:, :],
                    OP.subtract, OP.mult),
                    reads=("x2_sb", "bn2L2", "tcol"), writes=("xlntok0",))
                rows_o = xn_d.rearrange("(a q) c -> a q c", q=128)[ch]
                p.op("sync", lambda s, rows_o=rows_o: s.dma_start(rows_o, xn_tok),
                     reads=("xlntok0",), writes=("xn_dram",), inc=16)

        if not do_ffn:
            for ch in range(n_blocks * CPB):
                rows_i = x2_d.rearrange("(a q) c -> a q c", q=128)[ch]
                rows_o = y_out.rearrange("(a q) c -> a q c", q=128)[ch]
                p.op("scalar", lambda s, rows_i=rows_i: s.dma_start(fin_t[:, :], rows_i),
                     reads=("x2_dram", "y_sb"), writes=("y_sb",), inc=16)
                p.op("sync", lambda s, rows_o=rows_o: s.dma_start(rows_o, fin_t[:, :]),
                     reads=("y_sb",), writes=("y_out",), inc=16)
            p.emit(block, handles)
            return nc, dram

        p.op("gpsimd", lambda g: g.collective_compute(
            "AllGather", OP.bypass, replica_groups=[list(range(NCORE))],
            ins=[xn_d], outs=[xn_all[0:TOT, :]]),
            reads=("xn_dram",), writes=("xn_all",), sem_as="ag1", inc=1)

        src_sgu = dram["wsgu"].rearrange("(a b) c -> b a c", b=128)
        p.op("gpsimd", lambda g: g.dma_start(
            w_sgu_a.ap().rearrange("q (a c) -> q a c", c=C), src_sgu),
            reads=("w_o",), writes=("w_o",), inc=16)
        src_sd = dram["wsd"].rearrange("(a b) c -> b a c", b=128)
        p.op("gpsimd", lambda g: g.dma_start(
            w_sd_a.ap()[:, 0:4 * C].rearrange("q (a c) -> q a c", c=C), src_sd),
            reads=("xlnfm",), writes=("xlnfm",), inc=16)

        for B in range(n_blocks):
            tok0 = B * TB
            for k in range(KC):
                src = xn_d[tok0:tok0 + TB, k * 128:(k + 1) * 128]
                p.op("sync", lambda s, src=src, k=k: s.dma_start(
                    ynfm[:, k * TB:(k + 1) * TB], src, transpose=True),
                    reads=("xn_dram", "ynfm"), writes=("ynfm",), inc=16)
            for j in range(CPB):
                ch = B * CPB + j

                def fnR(t, j=j):
                    inst = None
                    for k in range(KC):
                        inst = t.matmul(
                            pbsm[:, 32:64],
                            ynfm[:, k * TB + j * 128: k * TB + (j + 1) * 128],
                            gwT[:, k * E:(k + 1) * E],
                            start=(k == 0), stop=(k == KC - 1))
                    return inst
                p.op("tensor", fnR, reads=("ynfm", "gwT"), writes=("pbsm",))
                p.op("vector", lambda v, ch=ch: v.tensor_copy(
                    logits_sb[:, ch * E:(ch + 1) * E], pbsm[:, 32:64]),
                    reads=("pbsm", "r_fm"), writes=("r_fm",))
            for k8 in range(KC):
                pt = pbb[k8 % 2]

                def fnG(t, k8=k8, pt=pt):
                    inst = None
                    for k in range(KC):
                        inst = t.matmul(
                            pt[:, :],
                            w_sgu_a[:, k * C + k8 * 128: k * C + k8 * 128 + 128],
                            ynfm[:, k * TB:(k + 1) * TB],
                            start=(k == 0), stop=(k == KC - 1))
                    return inst
                p.op("tensor", fnG, reads=("ynfm", "w_o"), writes=(f"pb{k8 % 2}",))
                if k8 < 4:
                    p.op("scalar", lambda s, k8=k8, pt=pt: s.activation(
                        h_fm[:, k8 * TB:(k8 + 1) * TB], pt[:, :], AF.Silu),
                        reads=(f"pb{k8 % 2}",), writes=("xxfm",))
                else:
                    p.op("vector", lambda v, k8=k8, pt=pt: v.tensor_mul(
                        h_fm[:, (k8 - 4) * TB:(k8 - 3) * TB],
                        h_fm[:, (k8 - 4) * TB:(k8 - 3) * TB], pt[:, :]),
                        reads=(f"pb{k8 % 2}", "xxfm"), writes=("xxfm",))
            for j in range(CPB):
                ch = B * CPB + j
                rt = rout_t[j % 2]
                rtk = f"rout{j % 2}"
                for half in range(2):
                    pt = pbb[half]

                    def fnD2(t, j=j, half=half, pt=pt):
                        inst = None
                        for k in range(4):
                            inst = t.matmul(
                                pt[:, :],
                                h_fm[:, k * TB + j * 128: k * TB + (j + 1) * 128],
                                w_sd_a[:, k * C + half * 512: k * C + (half + 1) * 512],
                                start=(k == 0), stop=(k == 3))
                        return inst
                    p.op("tensor", fnD2, reads=("xxfm", "xlnfm"), writes=(f"pb{half}",))
                    p.op("scalar", lambda s, half=half, pt=pt, rt=rt: s.activation(
                        rt[:, half * 512:(half + 1) * 512], pt[:, :], AF.Copy),
                        reads=(f"pb{half}",), writes=(rtk,))
                rows_o = sh_d.rearrange("(a q) c -> a q c", q=128)[ch]
                p.op("sync", lambda s, rows_o=rows_o, rt=rt: s.dma_start(rows_o, rt[:, :]),
                     reads=(rtk, "xf"), writes=("sh_dram",), inc=16)

        for ch in range(NCH):
            p.op("vector", lambda v, ch=ch: v.max(
                out=m8_sb[:, ch * 8:(ch + 1) * 8],
                in_=logits_sb[:, ch * E:(ch + 1) * E]),
                reads=("r_fm",), writes=("r_fm",))
        for ch in range(NCH):
            p.op("vector", lambda v, ch=ch: v.tensor_copy(
                thrT[:, ch:ch + 1], m8_sb[:, ch * 8 + 3:ch * 8 + 4]),
                reads=("r_fm",), writes=("r_fm",))
            p.op("vector", lambda v, ch=ch: v.tensor_copy(
                m1T[:, ch:ch + 1], m8_sb[:, ch * 8:ch * 8 + 1]),
                reads=("r_fm",), writes=("r_fm",))
        p.op("vector", lambda v: v.tensor_tensor(
            maskb.rearrange("q (a e) -> q a e", e=E),
            logits_sb.rearrange("q (a e) -> q a e", e=E),
            thrT.unsqueeze(2).to_broadcast([128, NCH, E]), OP.is_ge),
            reads=("r_fm", "v_tok"), writes=("v_tok",))
        p.op("vector", lambda v: v.tensor_tensor(
            gates_sb.rearrange("q (a e) -> q a e", e=E),
            logits_sb.rearrange("q (a e) -> q a e", e=E),
            m1T.unsqueeze(2).to_broadcast([128, NCH, E]), OP.subtract),
            reads=("r_fm", "k_fm"), writes=("k_fm",))
        p.op("scalar", lambda s: s.activation(gates_sb, gates_sb, AF.Exp),
             reads=("k_fm",), writes=("k_fm",))
        p.op("vector", lambda v: v.tensor_mul(gates_sb, gates_sb, maskb),
             reads=("k_fm", "v_tok"), writes=("k_fm",))
        p.op("vector", lambda v: v.tensor_reduce(
            thrT, gates_sb.rearrange("q (a e) -> q a e", e=E), AX.X, OP.add),
            reads=("k_fm", "r_fm"), writes=("r_fm",))
        p.op("vector", lambda v: v.reciprocal(thrT, thrT),
             reads=("r_fm",), writes=("r_fm",))
        p.op("vector", lambda v: v.tensor_mul(
            gates_sb.rearrange("q (a e) -> q a e", e=E),
            gates_sb.rearrange("q (a e) -> q a e", e=E),
            thrT.unsqueeze(2).to_broadcast([128, NCH, E])),
            reads=("k_fm", "r_fm"), writes=("k_fm",))

        cnts = x2_sb[0:1, 0:NCH * E]
        for ch in range(NCH):
            p.op("vector", lambda v, ch=ch: v.tensor_copy(
                maskf[:, :], maskb[:, ch * E:(ch + 1) * E]),
                reads=("v_tok",), writes=("maskf",))
            p.op("tensor", lambda t, ch=ch: t.matmul(
                pbsm[0:1, 0:E], onescol_f[:, :], maskf[:, :],
                start=True, stop=True),
                reads=("maskf", "onescol_f"), writes=("pbsm",))
            p.op("vector", lambda v, ch=ch: v.tensor_copy(
                cnts[:, ch * E:(ch + 1) * E], pbsm[0:1, 0:E]),
                reads=("pbsm",), writes=("x2_sb",))
        p.op("vector", lambda v: v.tensor_reduce(
            cntrow[:, :], cnts.rearrange("q (a b) -> q b a", b=E), AX.X, OP.add),
            reads=("x2_sb",), writes=("cntrow",))
        p.op("sync", lambda s: s.dma_start(cnt_d, cntrow[:, :]),
             reads=("cntrow",), writes=("cnt_dram",), inc=16)
        p.op("gpsimd", lambda g: g.collective_compute(
            "AllGather", OP.bypass, replica_groups=[list(range(NCORE))],
            ins=[cnt_d], outs=[cnt8_d]),
            reads=("cnt_dram",), writes=("cnt8_dram",), sem_as="ag0", inc=1)
        p.op("sync", lambda s: s.dma_start(cnt8_sb[:, :], cnt8_d),
             reads=("cnt8_dram",), writes=("cnt8_sb",), inc=16)
        p.op("tensor", lambda t: t.matmul(
            pbsm[0:1, 64:64 + E], ind_sb[:, :], cnt8_sb[:, :], start=True, stop=True),
            reads=("ind_sb", "cnt8_sb"), writes=("pbsm",))
        p.op("vector", lambda v: v.tensor_copy(
            chpfx[:, 0:E], pbsm[0:1, 64:64 + E]),
            reads=("pbsm",), writes=("chpfx",))
        for ch in range(1, NCH):
            p.op("vector", lambda v, ch=ch: v.tensor_add(
                chpfx[:, ch * E:(ch + 1) * E], chpfx[:, (ch - 1) * E:ch * E],
                cnts[:, (ch - 1) * E:ch * E]),
                reads=("chpfx", "x2_sb"), writes=("chpfx",))
        for ch in range(NCH):
            p.op("vector", lambda v, ch=ch: v.tensor_copy(
                maskf[:, :], maskb[:, ch * E:(ch + 1) * E]),
                reads=("v_tok",), writes=("maskf",))

            def fnPos(t, ch=ch):
                t.matmul(pbsm[:, 96:96 + E], stril_f[:, :], maskf[:, :],
                         start=True, stop=False)
                return t.matmul(pbsm[:, 96:96 + E], ones1f[0:1, :],
                                chpfx[0:1, ch * E:(ch + 1) * E],
                                start=False, stop=True)
            p.op("tensor", fnPos, reads=("maskf", "stril_f", "ones1f", "chpfx"),
                 writes=("pbsm",))
            p.op("vector", lambda v, ch=ch: v.tensor_add(
                posg_sb[:, ch * E:(ch + 1) * E], pbsm[:, 96:96 + E], iotak[:, :]),
                reads=("pbsm", "iotak", "k_fm"), writes=("k_fm",))

        p.op("vector", lambda v: v.tensor_tensor(
            keep_sb.rearrange("q (a e) -> q a e", e=E),
            posg_sb.rearrange("q (a e) -> q a e", e=E),
            iotak1024.ap().unsqueeze(1).to_broadcast([128, NCH, E]), OP.is_lt),
            reads=("k_fm", "iotak1024", "ruk_fm"), writes=("ruk_fm",))
        p.op("vector", lambda v: v.tensor_mul(keep_sb, keep_sb, maskb),
             reads=("ruk_fm", "v_tok"), writes=("ruk_fm",))
        p.op("vector", lambda v: v.scalar_tensor_tensor(
            tbl_sb, keep_sb, -BIG, posg_sb, OP.mult, OP.add),
            reads=("ruk_fm", "k_fm"), writes=("ruk_fm",))
        p.op("vector", lambda v: v.tensor_scalar(
            tbl_sb, tbl_sb, BIG, None, OP.add),
            reads=("ruk_fm",), writes=("ruk_fm",))
        p.op("vector", lambda v: v.tensor_mul(gates_sb, gates_sb, keep_sb),
             reads=("k_fm", "ruk_fm"), writes=("k_fm",))

        for ch in range(NCH):
            for kk in range(4):
                idx = ch * 4 + kk
                p.op("vector", lambda v, ch=ch, kk=kk, idx=idx: v.scalar_tensor_tensor(
                    sqscr[:, 0:E], logits_sb[:, ch * E:(ch + 1) * E],
                    m8_sb[:, ch * 8 + kk:ch * 8 + kk + 1],
                    tbl_sb[:, ch * E:(ch + 1) * E],
                    OP.is_equal, OP.mult,
                    accum_out=slotcols[:, idx:idx + 1]),
                    reads=("r_fm", "ruk_fm", "v_tok"), writes=("sqscr", "v_tok"))
                p.op("vector", lambda v, ch=ch, kk=kk, idx=idx: v.scalar_tensor_tensor(
                    sqscr[:, 0:E], logits_sb[:, ch * E:(ch + 1) * E],
                    m8_sb[:, ch * 8 + kk:ch * 8 + kk + 1],
                    gates_sb[:, ch * E:(ch + 1) * E],
                    OP.is_equal, OP.mult,
                    accum_out=gcols[:, idx:idx + 1]),
                    reads=("r_fm", "k_fm", "v_tok"), writes=("sqscr", "v_tok"))
        p.op("vector", lambda v: v.tensor_tensor(
            slotcols, slotcols, trash_iota[:, :], OP.min),
            reads=("v_tok", "trash_iota"), writes=("v_tok",))
        p.op("vector", lambda v: v.tensor_copy(slotints, slotcols),
             reads=("v_tok",), writes=("v_tok",))
        p.op("vector", lambda v: v.tensor_copy(
            sc_all.rearrange("q (a b t) -> q a b t", b=4, t=2)[:, :, :, 0],
            iota_tg1.ap().unsqueeze(2).to_broadcast([128, NCH, 4])),
            reads=("iota_tg1", "v_tok"), writes=("v_tok",))
        p.op("vector", lambda v: v.tensor_copy(
            sc_all.rearrange("q (a b t) -> q a b t", b=4, t=2)[:, :, :, 1],
            gcols.rearrange("q (a b) -> q a b", b=4)),
            reads=("v_tok",), writes=("v_tok",))
        for ch in range(NCH):
            for kk in range(4):
                idx = ch * 4 + kk
                chainkey = f"smw{kk}" if CC_WAIT == 16 else f"smw{kk}_{ch}"
                p.op("gpsimd", lambda g, idx=idx, kk=kk: g.indirect_dma_start(
                    slotmap4_d[kk], bass.IndirectOffsetOnAxis(slotints[:, idx:idx + 1], 0),
                    sc_all[:, 2 * idx:2 * idx + 2], None),
                    reads=("v_tok", f"smz{kk}", chainkey), writes=(chainkey,), inc=16)
        for kk in range(4):
            arreads = ((f"smw{kk}",) if CC_WAIT == 16 else
                       tuple(f"smw{kk}_{ch}" for ch in range(NCH)))
            p.op("gpsimd", lambda g, kk=kk: g.collective_compute(
                "AllReduce", OP.add, replica_groups=[list(range(NCORE))],
                ins=[slotmap4_d[kk][0:E * CAP, :]], outs=[slotmapG4_d[kk]]),
                reads=arreads, writes=("slotmapG",), sem_as="ar1", inc=1)

        for q in range(EPC):
            p.op("vector", lambda v, q=q: v.tensor_scalar(
                capf[:, q * 8:(q + 1) * 8], iotasch[:, :],
                meta_sb[:, 1 + q:2 + q], None, OP.add),
                reads=("iotasch", "meta_sb"), writes=("capf",))
        p.op("vector", lambda v: v.tensor_copy(capidx_all[:, :], capf[:, :]),
             reads=("capf",), writes=("capidx_all",))
        for i in range(EPC * 8):
            for kk in range(4):
                p.op("gpsimd", lambda g, i=i, kk=kk: g.indirect_dma_start(
                    smrows_all[:, 8 * i + 2 * kk:8 * i + 2 * kk + 2], None,
                    slotmapG4_d[kk], bass.IndirectOffsetOnAxis(capidx_all[:, i:i + 1], 0)),
                    reads=("capidx_all", "slotmapG"), writes=(f"smr{i}",), inc=16)
        smr_keys = tuple(f"smr{i}" for i in range(EPC * 8))
        sm4 = smrows_all.ap().rearrange("q (a kk t) -> q a t kk", kk=4, t=2)
        p.op("vector", lambda v: v.tensor_reduce(
            capf2_all.ap().rearrange("q (a t) -> q a t", t=2), sm4, AX.X, OP.add),
            reads=smr_keys, writes=("capf2_all",))
        sm2 = capf2_all.ap().rearrange("q (a t) -> q a t", t=2)
        p.op("vector", lambda v: v.tensor_scalar(
            capf[:, :], sm2[:, :, 0], 1.0, None, OP.subtract),
            reads=("capf2_all", "capidx_all"), writes=("capf",))
        p.op("vector", lambda v: v.tensor_copy(smg_all[:, :], sm2[:, :, 1]),
             reads=("capf2_all",), writes=("smg_all",))
        p.op("vector", lambda v: v.tensor_scalar(
            sqscr[:, 0:EPC * 8], capf[:, :], 0.0, None, OP.is_lt),
            reads=("capf",), writes=("sqscr",))
        p.op("vector", lambda v: v.scalar_tensor_tensor(
            capf[:, :], sqscr[:, 0:EPC * 8], float(TOT + 50), capf[:, :],
            OP.mult, OP.add),
            reads=("sqscr", "capf"), writes=("capf",))
        p.op("vector", lambda v: v.tensor_scalar(
            capf[:, :], capf[:, :], float(TOT), None, OP.min),
            reads=("capf",), writes=("capf",))
        p.op("vector", lambda v: v.tensor_copy(capidx_all[:, :], capf[:, :]),
             reads=("capf",), writes=("capidx_all",))
        for i in range(EPC * 8):
            bb_ = rout_t[i % NROUT]
            bbk = f"rout{i % NROUT}"
            p.op("gpsimd", lambda g, i=i, bb_=bb_: g.indirect_dma_start(
                bb_[:, :], None,
                xn_all, bass.IndirectOffsetOnAxis(capidx_all[:, i:i + 1], 0)),
                reads=("capidx_all", "xn_all", "xnall_z", bbk, "xf"),
                writes=(bbk,), inc=16)
            p.op("sync", lambda s, i=i, bb_=bb_: s.dma_start(
                cap_d[i * 128:(i + 1) * 128, :], bb_[:, :]),
                reads=(bbk, "xf"), writes=(f"capd{i}",), inc=16)

        for q in range(EPC):
            src_gu = dram["wgu_own"].rearrange("e (a b) c -> e b a c", b=128)[q]
            p.op("gpsimd", lambda g, src_gu=src_gu: g.dma_start(
                w_gu_e.ap().rearrange("q (a c) -> q a c", c=C), src_gu),
                reads=("w_r",), writes=("w_r",), inc=16)
            src_d = dram["wd_own"].rearrange("e (a b) c -> e b a c", b=128)[q]
            p.op("gpsimd", lambda g, src_d=src_d: g.dma_start(
                w_d_e[:, 0:4 * C].rearrange("q (a c) -> q a c", c=C), src_d),
                reads=("w_k",), writes=("w_k",), inc=16)
            for k in range(KC):
                p.op("sync", lambda s, q=q, k=k: s.dma_start(
                    capfm[:, k * CAP:(k + 1) * CAP],
                    cap_d[q * CAP:(q + 1) * CAP, k * 128:(k + 1) * 128], transpose=True),
                    reads=tuple(f"capd{q * 8 + sch}" for sch in range(8)) + ("w_v",),
                    writes=("w_v",), inc=16)
            for k8 in range(KC):
                for half in range(2):
                    pt = pbb[half]

                    def fnEG(t, k8=k8, half=half, pt=pt):
                        inst = None
                        for k in range(KC):
                            inst = t.matmul(
                                pt[:, :],
                                w_gu_e[:, k * C + k8 * 128: k * C + k8 * 128 + 128],
                                capfm[:, k * CAP + half * 512: k * CAP + (half + 1) * 512],
                                start=(k == 0), stop=(k == KC - 1))
                        return inst
                    p.op("tensor", fnEG, reads=("w_r", "w_v"), writes=(f"pb{half}",))
                    if k8 < 4:
                        p.op("scalar", lambda s, k8=k8, half=half, pt=pt: s.activation(
                            h_fm[:, k8 * CAP + half * 512: k8 * CAP + (half + 1) * 512],
                            pt[:, :], AF.Silu),
                            reads=(f"pb{half}",), writes=("xxfm",))
                    else:
                        p.op("vector", lambda v, k8=k8, half=half, pt=pt: v.tensor_mul(
                            h_fm[:, (k8 - 4) * CAP + half * 512:
                                 (k8 - 4) * CAP + (half + 1) * 512],
                            h_fm[:, (k8 - 4) * CAP + half * 512:
                                 (k8 - 4) * CAP + (half + 1) * 512],
                            pt[:, :]),
                            reads=(f"pb{half}", "xxfm"), writes=("xxfm",))
            for sch in range(8):
                rt = rout_t[sch % 2]
                rtk = f"rout{sch % 2}"
                for half in range(2):
                    pt = pbb[half]

                    def fnED(t, sch=sch, half=half, pt=pt):
                        inst = None
                        for k in range(4):
                            inst = t.matmul(
                                pt[:, :],
                                h_fm[:, k * CAP + sch * 128: k * CAP + (sch + 1) * 128],
                                w_d_e[:, k * C + half * 512: k * C + (half + 1) * 512],
                                start=(k == 0), stop=(k == 3))
                        return inst
                    p.op("tensor", fnED, reads=("xxfm", "w_k"), writes=(f"pb{half}",))
                    p.op("scalar", lambda s, q=q, sch=sch, half=half, pt=pt, rt=rt: s.mul(
                        rt[:, half * 512:(half + 1) * 512], pt[:, :],
                        smg_all[:, q * 8 + sch:q * 8 + sch + 1]),
                        reads=(f"pb{half}", "smg_all"), writes=(rtk,))
                rows_o = eout_d.rearrange("(a q) c -> a q c", q=128)[q * 8 + sch]
                p.op("sync", lambda s, rows_o=rows_o, rt=rt: s.dma_start(rows_o, rt[:, :]),
                     reads=(rtk, "xf"), writes=("eout_dram",), inc=16)

        p.op("gpsimd", lambda g: g.collective_compute(
            "AllGather", OP.bypass, replica_groups=[list(range(NCORE))],
            ins=[eout_d], outs=[eoutG_d[0:E * CAP, :]]),
            reads=("eout_dram",), writes=("eoutG",), sem_as="ag2", inc=1)

        for w0 in range(0, NCH, NROUT):
            wave = list(range(w0, min(w0 + NROUT, NCH)))
            for ch in wave:
                rt = rout_t[ch % NROUT]
                rtk = f"rout{ch % NROUT}"
                rows_i = sh_d.rearrange("(a q) c -> a q c", q=128)[ch]
                p.op("scalar", lambda s, rows_i=rows_i, rt=rt: s.dma_start(rt[:, :], rows_i),
                     reads=("sh_dram", rtk, "xf"), writes=(rtk,), inc=16)
            for kk in range(4):
                for ch in wave:
                    rt = rout_t[ch % NROUT]
                    rtk = f"rout{ch % NROUT}"
                    idx = ch * 4 + kk
                    p.op("gpsimd", lambda g, idx=idx, rt=rt: g.indirect_dma_start(
                        rt[:, :], None,
                        eoutG_d, bass.IndirectOffsetOnAxis(slotints[:, idx:idx + 1], 0),
                        compute_op=OP.add),
                        reads=("v_tok", "eoutG", "eoutG_z", rtk), writes=(rtk,),
                        inc=16)
            for ch in wave:
                rt = rout_t[ch % NROUT]
                rtk = f"rout{ch % NROUT}"
                p.op("vector", lambda v, rt=rt: v.bn_stats(bn6[:, 36:42], rt[:, 0:512]),
                     reads=(rtk, "xf"), writes=("bn6R",))
                p.op("vector", lambda v, rt=rt: v.bn_stats(bn6[:, 42:48], rt[:, 512:1024]),
                     reads=(rtk, "xf"), writes=("bn6R",))
                p.op("vector", lambda v: v.bn_aggr(bn2[:, 6:8], bn6[:, 36:48]),
                     reads=("bn6R",), writes=("bn2R",))
                p.op("scalar", lambda s: s.activation(
                    tcol[:, :], bn2[:, 7:8], AF.Sqrt, bias=LN_EPS, scale=1.0),
                    reads=("bn2R", "consts"), writes=("tcol",))
                p.op("vector", lambda v: v.reciprocal(tcol[:, :], tcol[:, :]),
                     reads=("tcol",), writes=("tcol",))
                rows_x2 = x2_d.rearrange("(a q) c -> a q c", q=128)[ch]
                p.op("scalar", lambda s, rows_x2=rows_x2: s.dma_start(fin_t[:, :], rows_x2),
                     reads=("x2_dram", "y_sb"), writes=("y_sb",), inc=16)
                p.op("vector", lambda v, rt=rt: v.tensor_scalar(
                    x2_sb[:, :], rt[:, :], bn2[:, 6:7], tcol[:, :],
                    OP.subtract, OP.mult),
                    reads=(rtk, "bn2R", "tcol", "x2_sb"), writes=("x2_sb",))
                p.op("vector", lambda v: v.tensor_add(
                    fin_t[:, :], fin_t[:, :], x2_sb[:, :]),
                    reads=("y_sb", "x2_sb"), writes=("y_sb",))
                rows_o = y_out.rearrange("(a q) c -> a q c", q=128)[ch]
                p.op("sync", lambda s, rows_o=rows_o: s.dma_start(rows_o, fin_t[:, :]),
                     reads=("y_sb",), writes=("y_out",), inc=16)

        p.emit(block, handles)
    return nc, dram


def make_in_maps(inputs):
    xq = np.ascontiguousarray(np.asarray(inputs["L0_attn_query"], np.float32))
    B_, T_, C_ = xq.shape
    flat = xq.reshape(B_ * T_, C_)

    def core_map(c):
        return {
            "x": np.ascontiguousarray(flat[c * T_CORE:(c + 1) * T_CORE]),
            "maa_x": np.asarray(inputs["L0_attn_time_maa_x"], np.float32),
            "maa_w": np.asarray(inputs["L0_attn_time_maa_w"], np.float32),
            "maa_k": np.asarray(inputs["L0_attn_time_maa_k"], np.float32),
            "maa_v": np.asarray(inputs["L0_attn_time_maa_v"], np.float32),
            "maa_r": np.asarray(inputs["L0_attn_time_maa_r"], np.float32),
            "t_decay": np.asarray(inputs["L0_attn_time_decay"], np.float32),
            "ln_w": np.asarray(inputs["L0_attn_ln_w"], np.float32),
            "ln_b": np.asarray(inputs["L0_attn_ln_b"], np.float32),
            "maa_w1": np.asarray(inputs["L0_attn_time_maa_w1"], np.float32),
            "maa_w2": np.asarray(inputs["L0_attn_time_maa_w2"], np.float32),
            "dw1": np.asarray(inputs["L0_attn_time_decay_w1"], np.float32),
            "dw2": np.asarray(inputs["L0_attn_time_decay_w2"], np.float32),
            "faaaa": np.asarray(inputs["L0_attn_time_faaaa"], np.float32),
            "W_r": np.asarray(inputs["L0_attn_W_r"], np.float32),
            "W_k": np.asarray(inputs["L0_attn_W_k"], np.float32),
            "W_v": np.asarray(inputs["L0_attn_W_v"], np.float32),
            "W_o": np.asarray(inputs["L0_attn_W_o"], np.float32),
            "gate_w": np.asarray(inputs["L0_ffn_gate_w"], np.float32),
            "wgu_own": np.ascontiguousarray(
                np.asarray(inputs["L0_ffn_w_gate_up"], np.float32)[4 * c:4 * c + 4]),
            "wd_own": np.ascontiguousarray(
                np.asarray(inputs["L0_ffn_w_down"], np.float32)[4 * c:4 * c + 4]),
            "wsgu": np.asarray(inputs["L0_ffn_shared_w_gate_up"], np.float32),
            "wsd": np.asarray(inputs["L0_ffn_shared_w_down"], np.float32),
            "meta_ind": (np.arange(8)[:, None] < c).astype(np.float32),
            "meta_cols": np.ascontiguousarray(np.broadcast_to(np.array(
                [c * T_CORE, (4 * c) * CAP, (4 * c + 1) * CAP,
                 (4 * c + 2) * CAP, (4 * c + 3) * CAP, 0, 0, 0],
                np.float32), (128, 8))),
        }
    return [core_map(c) for c in range(NCORE)], (B_, T_, C_)


def kernel(**inputs):
    in_maps, (B_, T_, C_) = make_in_maps(inputs)
    nc, _ = build_nc()
    res = run_bass_kernel_spmd(nc, in_maps, core_ids=list(range(NCORE)))
    out = np.concatenate([res.results[c]["y"] for c in range(NCORE)], axis=0)
    return out.reshape(B_, T_, C_).astype(np.float32)
